# version 06
# baseline (speedup 1.0000x reference)
import numpy as np
import ml_dtypes
import concourse.bass as bass
import concourse.mybir as mybir
from concourse.bass_utils import run_bass_kernel_spmd
from contextlib import ExitStack

ALU = mybir.AluOpType
AX = mybir.AxisListType
AF = mybir.ActivationFunctionType
F32 = mybir.dt.float32
BF16 = mybir.dt.bfloat16
U16 = mybir.dt.uint16
U8 = mybir.dt.uint8
NPBF16 = ml_dtypes.bfloat16

T, HID = 8192, 2048
H, HD = 16, 128
M = 16
NB = 512
WIN = 256
NCORES = 8
TSL = 1024
BND = 256
NT = BND + TSL
EPS = 1e-6
LNEPS = 1e-5
EPS_LAD = 1e-30
SCALE = float(np.float32(1.0) / np.sqrt(np.float64(HD)).astype(np.float32))

NO_COLLECTIVE = False
TAPS = []


def _tile_lhst(w):
    K, Mo = w.shape
    return np.ascontiguousarray(
        w.reshape(K // 128, 128, Mo // 128, 128).transpose(2, 1, 0, 3))


def _rope_tables():
    import jax
    import jax.numpy as jnp
    cpu = jax.devices("cpu")[0]
    with jax.default_device(cpu):
        inv = 1.0 / 10000.0 ** (jnp.arange(0, 64, 2, dtype=jnp.float32) / 64)
        ang = jnp.arange(T, dtype=jnp.float32)[:, None] * inv
        return np.asarray(jnp.cos(ang)), np.asarray(jnp.sin(ang))


def _host_inputs(inputs):
    f32 = np.float32
    x = np.asarray(inputs["L0_attn_h"]).reshape(T, HID).astype(f32)

    wsmall = np.zeros((HID, 768), f32)
    wsmall[:, 0:128] = inputs["L0_attn_kvc_w"]
    wsmall[:, 128:256] = inputs["L0_attn_kvc_wz"]
    wsmall[:, 256:384] = inputs["L0_attn_k_proj_w"]
    wsmall[:, 384:512] = inputs["L0_attn_v_proj_w"]
    wsmall[:, 512:576] = inputs["L0_attn_idx_c_w"]
    wsmall[:, 576:640] = inputs["L0_attn_w_dq"]
    wsmall[:, 640:704] = inputs["L0_attn_idx_c_wz"]
    wsmall[:, 704:708] = inputs["L0_attn_w_w"]

    shared = {
        "wsmall": _tile_lhst(wsmall),
        "wqc": _tile_lhst(np.asarray(inputs["L0_attn_w_qc"], f32)),
        "wqup": _tile_lhst(np.asarray(inputs["L0_attn_w_qup"], f32)),
        "wiuq": np.ascontiguousarray(np.asarray(inputs["L0_attn_w_iuq"], f32)),
        "wgroup": np.stack([_tile_lhst(np.asarray(inputs["L0_attn_group_w"], f32)[g])
                            for g in range(4)]),
        "wfinal": _tile_lhst(np.asarray(inputs["L0_attn_final_w"], f32)),
        "wg_t": _tile_lhst(np.asarray(inputs["L0_ffn_gate_w"]).astype(np.uint16).T),
        "wu_t": _tile_lhst(np.asarray(inputs["L0_ffn_up_w"]).astype(np.uint16).T),
        "wd_t": _tile_lhst(np.asarray(inputs["L0_ffn_down_w"]).astype(np.uint16).T),
        "gsc": np.ascontiguousarray(np.asarray(inputs["L0_ffn_gate_scale"], f32).reshape(64, 128).T),
        "usc": np.ascontiguousarray(np.asarray(inputs["L0_ffn_up_scale"], f32).reshape(64, 128).T),
        "dsc": np.ascontiguousarray(np.asarray(inputs["L0_ffn_down_scale"], f32).reshape(16, 128).T),
        "qnw": np.asarray(inputs["L0_attn_q_norm_w"], f32).reshape(128, 1),
        "knw": np.asarray(inputs["L0_attn_k_norm_w"], f32).reshape(128, 1),
        "kvcb": np.ascontiguousarray(np.asarray(inputs["L0_attn_kvc_bias"], f32).T),
        "idxb": np.ascontiguousarray(np.asarray(inputs["L0_attn_idx_c_bias"], f32).T),
    }

    actc = np.zeros((128, 8), f32)
    actc[:, 0] = LNEPS; actc[:, 1] = -1.0; actc[:, 2] = EPS; actc[:, 3] = 1.0 / HD
    shared["actc"] = actc
    shared["idb"] = np.eye(128, dtype=NPBF16)
    shared["idf"] = np.eye(128, dtype=f32)
    shared["onescb"] = np.ones((128, 1), NPBF16)
    shared["onesrf"] = np.ones((1, 128), f32)
    shared["niota"] = np.broadcast_to(np.arange(NB, dtype=f32), (128, NB)).copy()
    lad = (EPS_LAD * (NB - 1 - np.arange(NB, dtype=np.float64))).astype(f32)
    shared["ladder"] = np.broadcast_to(lad, (128, NB)).copy()
    shared["neglad"] = np.broadcast_to((np.float32(-1e30) + lad).astype(f32), (128, NB)).copy()

    i = np.arange(WIN)[:, None]
    j = np.arange(2 * WIN)[None, :]
    mgen = ((j - i > 0) & (j - i <= WIN))
    m0 = (mgen & (j >= WIN))

    def packmask(mk):
        mt = mk.T.astype(NPBF16)
        return np.ascontiguousarray(mt.reshape(4, 128, WIN).transpose(1, 0, 2))

    shared["bwg"] = packmask(mgen)
    bw0 = packmask(m0)

    cosT, sinT = _rope_tables()

    in_maps = []
    for c in range(NCORES):
        t0 = c * TSL
        xs = np.zeros((NT, HID), f32)
        cs = np.zeros((NT, 32), f32)
        sn = np.zeros((NT, 32), f32)
        lo = t0 - BND
        if lo >= 0:
            xs[:] = x[lo:t0 + TSL]
            cs[:] = cosT[lo:t0 + TSL]
            sn[:] = sinT[lo:t0 + TSL]
        else:
            xs[BND:] = x[t0:t0 + TSL]
            cs[BND:] = cosT[t0:t0 + TSL]
            sn[BND:] = sinT[t0:t0 + TSL]

        tg = np.arange(t0, t0 + TSL)
        nv = np.where(tg >= M, tg // M, 0).astype(f32)

        m = dict(shared)
        m["x"] = xs
        m["cos2"] = np.ascontiguousarray(np.concatenate([cs.T, cs.T], 0))
        m["sin2"] = np.ascontiguousarray(np.concatenate([sn.T, sn.T], 0))
        m["nvcol"] = np.ascontiguousarray(nv.reshape(8, 128).T)
        m["zrow"] = (nv > 0).astype(f32).reshape(1, TSL)
        m["f0col"] = np.ascontiguousarray((nv == 0).astype(f32).reshape(8, 128).T)
        m["bw0"] = bw0 if c == 0 else shared["bwg"]
        in_maps.append(m)
    return in_maps


def build_nc(debug_taps=()):
    nc = bass.Bass("TRN2", target_bir_lowering=False, detect_race_conditions=False)
    ap = {}

    def din(name, shape, dtype=F32):
        ap[name] = nc.dram_tensor(name, list(shape), dtype, kind="ExternalInput").ap()

    din("x", (NT, HID))
    din("wsmall", (6, 128, 16, 128)); din("wqc", (4, 128, 16, 128))
    din("wqup", (16, 128, 4, 128)); din("wiuq", (64, 256))
    din("wgroup", (4, 4, 128, 4, 128)); din("wfinal", (16, 128, 16, 128))
    din("wg_t", (64, 128, 16, 128), U16); din("wu_t", (64, 128, 16, 128), U16)
    din("wd_t", (16, 128, 64, 128), U16)
    din("gsc", (128, 64)); din("usc", (128, 64)); din("dsc", (128, 16))
    din("qnw", (128, 1)); din("knw", (128, 1))
    din("kvcb", (128, 16)); din("idxb", (64, 16))
    din("idf", (128, 128)); din("idb", (128, 128), BF16)
    din("onescb", (128, 1), BF16); din("onesrf", (1, 128))
    din("niota", (128, NB)); din("ladder", (128, NB)); din("neglad", (128, NB))
    din("bwg", (128, 4, WIN), BF16); din("bw0", (128, 4, WIN), BF16)
    din("cos2", (64, NT)); din("sin2", (64, NT))
    din("nvcol", (128, 8)); din("zrow", (1, TSL)); din("f0col", (128, 8))
    din("actc", (128, 8))

    out = nc.dram_tensor("out", [TSL, HID], F32, kind="ExternalOutput").ap()
    x1d = nc.dram_tensor("x1scratch", [TSL, HID], F32, kind="Internal").ap()
    cc_in = nc.dram_tensor("cc_in", [128, 128], F32, kind="Internal").ap()
    cc_out = nc.dram_tensor("cc_out", [1024, 128], F32, kind="Internal").ap()
    tap_outs = {}
    for tname, tshape, tdt in debug_taps:
        tap_outs[tname] = nc.dram_tensor("tap_" + tname, list(tshape), tdt,
                                         kind="ExternalOutput").ap()

    V, Sc, Te, Sy, Gp = nc.vector, nc.scalar, nc.tensor, nc.sync, nc.gpsimd

    ctx = ExitStack()
    with ctx:
        e = ctx.enter_context

        class Sem:
            def __init__(self, name):
                self.h = e(nc.semaphore(name)); self.n = 0
            def inc(self, inst, k=1):
                inst.then_inc(self.h, k); self.n += k

        ldx = Sem("s_ldx"); ldw = Sem("s_ldw"); ldf = Sem("s_ldf"); ldv = Sem("s_ldv")
        pe = Sem("s_pe"); dv = Sem("s_dv"); ac = Sem("s_ac"); gp = Sem("s_gp")
        st = Sem("s_st")
        blk = e(nc.Block())

        def tap(name, src_ap):
            if name in tap_outs:
                Sy.wait_ge(dv.h, dv.n); Sy.wait_ge(ac.h, ac.n); Sy.wait_ge(pe.h, pe.n)
                st.inc(Sy.dma_start(out=tap_outs[name], in_=src_ap), 16)

        C = {}
        cshapes = [("qnw", (128, 1), F32), ("knw", (128, 1), F32),
                   ("kvcb", (128, 16), F32), ("idxb", (64, 16), F32),
                   ("idf", (128, 128), F32), ("idb", (128, 128), BF16),
                   ("onescb", (128, 1), BF16), ("onesrf", (1, 128), F32),
                   ("nvcol", (128, 8), F32), ("zrow", (1, TSL), F32),
                   ("f0col", (128, 8), F32), ("gsc", (128, 64), F32),
                   ("usc", (128, 64), F32), ("dsc", (128, 16), F32),
                   ("bwg", (128, 4 * WIN), BF16), ("bw0", (128, 4 * WIN), BF16),
                   ("wiuq", (64, 256), F32),
                   ("actc", (128, 8), F32)]
        for nm, shp, dt in cshapes:
            C[nm] = e(nc.sbuf_tensor("c_" + nm, [shp[0], int(np.prod(shp[1:]))], dt))
        qwk = e(nc.sbuf_tensor("qwk", [128, 1], F32))
        junk = e(nc.sbuf_tensor("junk", [128, 1], F32))
        rowA = e(nc.sbuf_tensor("rowA", [1, 1280], F32))
        rowB = e(nc.sbuf_tensor("rowB", [1, 1280], F32))

        cdctx = ExitStack()
        ce0 = cdctx.enter_context
        qT = ce0(nc.sbuf_tensor("qT", [128, H * TSL], BF16))
        kT = ce0(nc.sbuf_tensor("kT", [128, NT], BF16))
        vrows = ce0(nc.sbuf_tensor("vrows", [128, 10 * 128], BF16))
        kvnT = ce0(nc.sbuf_tensor("kvnT", [128, NB], BF16))
        kvnr = ce0(nc.sbuf_tensor("kvnr", [128, 4 * 128], BF16))
        BTm = ce0(nc.sbuf_tensor("BTm", [128, 4 * TSL], BF16))

        abctx = ExitStack()
        abe = abctx.enter_context
        cshapes2 = [("niota", (128, NB), F32), ("ladder", (128, NB), F32),
                    ("neglad", (128, NB), F32), ("cos2", (64, NT), F32),
                    ("sin2", (64, NT), F32)]
        for nm, shp, dt in cshapes2:
            C[nm] = abe(nc.sbuf_tensor("c_" + nm, [shp[0], int(np.prod(shp[1:]))], dt))
        for nm, shp, dt in cshapes + cshapes2:
            src = ap[nm]
            if len(src.shape) == 3:
                src = src.rearrange("a b c -> a (b c)")
            ldv.inc(Sc.dma_start(out=C[nm][:, :], in_=src), 16)

        V.wait_ge(ldv.h, ldv.n)
        Sc.wait_ge(ldv.h, ldv.n)
        Te.wait_ge(ldv.h, ldv.n)
        V.tensor_scalar(out=qwk[:], in0=C["qnw"][:], scalar1=SCALE, scalar2=None,
                        op0=ALU.mult)

        def wait_all_compute(eng):
            eng.wait_ge(dv.h, dv.n); eng.wait_ge(ac.h, ac.n); eng.wait_ge(pe.h, pe.n)

        def vspace():
            V.tensor_copy(out=junk[:], in_=qwk[:])

        def barrier():
            for eng in (V, Sc, Te, Sy, Gp):
                wait_all_compute(eng)
                for sm in (st, ldx, ldw, ldf, ldv, gp):
                    eng.wait_ge(sm.h, sm.n)

        xv = ap["x"].rearrange("(i p) h -> i p h", p=128)

        xlnT = abe(nc.sbuf_tensor("xlnT", [128, 16 * NT], BF16))

        with ExitStack() as actx:
            ae = actx.enter_context
            xa = ae(nc.sbuf_tensor("xa", [128, 2, HID], F32))
            xlnn = ae(nc.sbuf_tensor("xlnn", [128, 2, HID], BF16))
            sc1 = ae(nc.sbuf_tensor("sc1", [128, 2, HID], F32))
            stat = ae(nc.sbuf_tensor("stat", [128, 2, 8], F32))
            ptr_a = ae(nc.psum_tensor("ptr_a", [128, 512], BF16))

            for i in range(10):
                par = i % 2
                Sy.wait_ge(dv.h, dv.n)
                ldx.inc(Sy.dma_start(out=xa[:, par, :], in_=xv[i]), 16)
                V.wait_ge(ldx.h, 16 * (i + 1))
                xi = xa[:, par, :]
                S = lambda k: stat[:, par, k:k + 1]
                V.tensor_reduce(out=S(0), in_=xi, axis=AX.X, op=ALU.add)
                V.scalar_tensor_tensor(out=sc1[:, par, :], in0=xi, scalar=1.0, in1=xi,
                                       op0=ALU.mult, op1=ALU.mult, accum_out=S(1))
                V.tensor_scalar(out=S(2), in0=S(0), scalar1=1.0 / HID, scalar2=None,
                                op0=ALU.mult)
                V.tensor_scalar(out=S(3), in0=S(1), scalar1=1.0 / HID, scalar2=None,
                                op0=ALU.mult)
                dv.inc(V.scalar_tensor_tensor(out=S(4), in0=S(2), scalar=S(2), in1=S(3),
                                               op0=ALU.mult, op1=ALU.subtract))
                Sc.wait_ge(dv.h, dv.n)
                ac.inc(Sc.activation(out=S(6), in_=S(4), func=AF.Sqrt,
                                     bias=C["actc"][:, 0:1], scale=C["actc"][:, 1:2]))
                V.wait_ge(ac.h, ac.n)
                V.reciprocal(out=S(7), in_=S(6))
                vspace()
                dv.inc(V.tensor_scalar(out=xlnn[:, par, :], in0=xi, scalar1=S(2),
                                       scalar2=S(7), op0=ALU.subtract, op1=ALU.mult))
                for cb in range(4):
                    Te.wait_ge(dv.h, dv.n)
                    if cb > 0 or i > 0:
                        Te.wait_ge(dv.h, dv.n)
                    for u in range(4):
                        hck = cb * 4 + u
                        inst = Te.transpose(ptr_a[:, u * 128:(u + 1) * 128],
                                            in_=xlnn[:, par, 128 * hck:128 * (hck + 1)],
                                            identity=C["idb"][:, :])
                    pe.inc(inst)
                    V.wait_ge(pe.h, pe.n)
                    for u in range(4):
                        hck = cb * 4 + u
                        inst = V.tensor_copy(
                            out=xlnT[:, hck * NT + 128 * i: hck * NT + 128 * (i + 1)],
                            in_=ptr_a[:, u * 128:(u + 1) * 128])
                    dv.inc(inst)

        barrier()
        barrier()
        with ExitStack() as bctx:
            be = bctx.enter_context
            projT = be(nc.sbuf_tensor("projT", [128, 6 * NT], F32))
            kcT = be(nc.sbuf_tensor("kcT", [64, NB], F32))
            kcTb = be(nc.sbuf_tensor("kcTb", [64, NB], BF16))
            ckvT = be(nc.sbuf_tensor("ckvT", [128, NB], F32))
            wstack = ExitStack()
            wsb = wstack.enter_context(nc.sbuf_tensor("wsb", [128, 2, 2048], F32, side="right"))
            wsb16 = wstack.enter_context(nc.sbuf_tensor("wsb16", [128, 2, 2048], BF16, side="right"))
            qcT = be(nc.sbuf_tensor("qcT", [128, 4 * TSL], BF16))
            qiT = be(nc.sbuf_tensor("qiT", [64, 4 * TSL], BF16))
            wihc = be(nc.sbuf_tensor("wihc", [128, 8 * 4], F32))

            def stream_w(name, mi, nk, parity, extra_idx=None):
                src = ap[name]
                if extra_idx is not None:
                    src = src[extra_idx]
                src = src.rearrange("m p k c -> m p (k c)")[mi]
                Sy.wait_ge(pe.h, pe.n)
                ldw.inc(Sy.dma_start(out=wsb[:, parity, 0:nk * 128], in_=src), 16)

            def conv_w(nk, parity):
                dv.inc(V.tensor_copy(out=wsb16[:, parity, 0:nk * 128],
                                     in_=wsb[:, parity, 0:nk * 128]))

            def wtile(parity, k):
                return wsb16[:, parity, 128 * k:128 * (k + 1)]

            barrier()
            with ExitStack() as b1:
                pj = b1.enter_context(nc.psum_tensor("pj", [128, 3 * 512], F32))
                stream_w("wsmall", 0, 16, 0)
                for m in range(6):
                    if m + 1 < 6:
                        stream_w("wsmall", m + 1, 16, (m + 1) % 2)
                    V.wait_ge(ldw.h, 16 * (m + 1))
                    conv_w(16, m % 2)
                    Te.wait_ge(dv.h, dv.n)
                    if m > 0:
                        Te.wait_ge(dv.h, dv.n)
                    for ntile in range(3):
                        ncols = 512 if ntile < 2 else 256
                        for k in range(16):
                            inst = Te.matmul(pj[:, 512 * ntile:512 * ntile + ncols],
                                             lhsT=wtile(m % 2, k),
                                             rhs=xlnT[:, k * NT + 512 * ntile:
                                                      k * NT + 512 * ntile + ncols],
                                             start=(k == 0), stop=(k == 15))
                    pe.inc(inst)
                    V.wait_ge(pe.h, pe.n)
                    psrc = pj[:, 0:NT]
                    dstv = projT[:, m * NT:(m + 1) * NT]
                    if m == 1:
                        bb = C["kvcb"][:, :].unsqueeze(1)
                        inst = V.tensor_tensor(
                            out=dstv.rearrange("p (nb mm) -> p nb mm", mm=16),
                            in0=psrc.rearrange("p (nb mm) -> p nb mm", mm=16),
                            in1=bb.broadcast_to([128, NT // 16, 16]), op=ALU.add)
                    elif m == 5:
                        V.tensor_copy(out=dstv[64:128, :], in_=psrc[64:128, :])
                        bb = C["idxb"][:, :].unsqueeze(1)
                        inst = V.tensor_tensor(
                            out=dstv[0:64, :].rearrange("p (nb mm) -> p nb mm", mm=16),
                            in0=psrc[0:64, :].rearrange("p (nb mm) -> p nb mm", mm=16),
                            in1=bb.broadcast_to([64, NT // 16, 16]), op=ALU.add)
                    else:
                        inst = V.tensor_copy(out=dstv, in_=psrc)
                    dv.inc(inst)

            barrier()
            with ExitStack() as b2:
                cmp_s = b2.enter_context(nc.sbuf_tensor("cmp_s", [128, TSL], F32))
                cmp_w = b2.enter_context(nc.sbuf_tensor("cmp_w", [128, 64], F32))
                ckv_o = b2.enter_context(nc.sbuf_tensor("ckv_o", [128, 64], F32))
                kc_o = b2.enter_context(nc.sbuf_tensor("kc_o", [64, 64], F32))

                def compress(chc, r0c, chz, r0z, rows, out_sb):
                    cview = projT[r0c:r0c + rows, chc * NT + BND:(chc + 1) * NT] \
                        .rearrange("p (nb mm) -> p nb mm", mm=16)
                    zview = projT[r0z:r0z + rows, chz * NT + BND:(chz + 1) * NT] \
                        .rearrange("p (nb mm) -> p nb mm", mm=16)
                    zm = cmp_w[r0z:r0z + rows, 0:64]
                    sview = cmp_s[r0z:r0z + rows, :].rearrange("p (nb mm) -> p nb mm", mm=16)
                    V.tensor_reduce(out=zm, in_=zview, axis=AX.X, op=ALU.max)
                    vspace()
                    dv.inc(V.tensor_tensor(
                        out=sview, in0=zview,
                        in1=zm.unsqueeze(2).broadcast_to([rows, 64, 16]),
                        op=ALU.subtract))
                    Sc.wait_ge(dv.h, dv.n)
                    ac.inc(Sc.activation(out=sview, in_=sview, func=AF.Exp,
                                         bias=0.0, scale=1.0))
                    V.wait_ge(ac.h, ac.n)
                    V.tensor_reduce(out=zm, in_=sview, axis=AX.X, op=ALU.add)
                    V.tensor_tensor(out=sview, in0=sview, in1=cview, op=ALU.mult)
                    V.reciprocal(out=zm, in_=zm)
                    V.tensor_reduce(out=out_sb, in_=sview, axis=AX.X, op=ALU.add)
                    vspace()
                    dv.inc(V.tensor_tensor(out=out_sb, in0=out_sb, in1=zm, op=ALU.mult))

                compress(0, 0, 1, 0, 128, ckv_o[:, :])
                compress(4, 0, 5, 0, 64, kc_o[:, :])

                V.memset(cmp_s[:, 0:32], 0.0)
                dv.inc(V.tensor_copy(out=cmp_s[:, 32:64], in_=cmp_s[:, 0:32]))
                Sy.wait_ge(dv.h, dv.n)
                ldw.inc(Sy.dma_start(out=cc_in[:, 96:128], in_=cmp_s[:, 0:32]), 16)
                ldw.inc(Sy.dma_start(out=cc_in[:, 0:64], in_=ckv_o[:, :]), 16)
                ldw.inc(Sy.dma_start(
                    out=cc_in[:, 64:96].rearrange("(hh p) l -> p hh l", hh=2),
                    in_=kc_o[:, :].rearrange("p (hh l) -> p hh l", hh=2)), 16)
                Gp.wait_ge(ldw.h, ldw.n)
                if NO_COLLECTIVE:
                    gp.inc(Gp.dma_start(out=cc_out[0:128, :], in_=cc_in), 16)
                else:
                    gp.inc(Gp.collective_compute(
                        "AllGather", ALU.bypass, replica_groups=[list(range(NCORES))],
                        ins=[cc_in.opt()], outs=[cc_out.opt()]), 1)
                Sy.wait_ge(gp.h, gp.n)
                ldw.inc(Sy.dma_start(
                    out=ckvT[:, :].rearrange("p (r f) -> p r f", r=8),
                    in_=cc_out.rearrange("(r p) f -> p r f", p=128)[:, :, 0:64]), 16)
                ldw.inc(Sy.dma_start(
                    out=kcT[:, :].rearrange("p (r hh l) -> p r hh l", r=8, hh=2),
                    in_=cc_out[:, 64:96].rearrange("(r hh p) l -> p r hh l", r=8, hh=2)), 16)
                GATHER_LD = ldw.n

            barrier()
            with ExitStack() as b3:
                pq = b3.enter_context(nc.psum_tensor("pq", [128, 2, 512], F32))
                psums = b3.enter_context(nc.psum_tensor("psums", [1, 512], F32))
                pbc = b3.enter_context(nc.psum_tensor("pbc", [128, 512], F32))
                qr_s = b3.enter_context(nc.sbuf_tensor("qr_s", [128, 512], F32))
                tmpA = b3.enter_context(nc.sbuf_tensor("tmpA", [128, 512], F32))
                tmpB = b3.enter_context(nc.sbuf_tensor("tmpB", [128, 512], F32))
                qsq = b3.enter_context(nc.sbuf_tensor("qsq", [128, 512], BF16))

                for m in range(4):
                    stream_w("wqc", m, 16, m % 2)
                    V.wait_ge(ldw.h, ldw.n)
                    conv_w(16, m % 2)
                    Te.wait_ge(dv.h, dv.n)
                    for nt in range(2):
                        for k in range(16):
                            inst = Te.matmul(pq[:, nt, :], lhsT=wtile(m % 2, k),
                                             rhs=xlnT[:, k * NT + BND + 512 * nt:
                                                      k * NT + BND + 512 * (nt + 1)],
                                             start=(k == 0), stop=(k == 15))
                    pe.inc(inst)
                    V.wait_ge(pe.h, pe.n)
                    dv.inc(V.tensor_copy(out=qcT[:, m * TSL:(m + 1) * TSL],
                                         in_=pq[:, :, :].rearrange("p a b -> p (a b)")))

                for hh in range(H):
                    stream_w("wqup", hh, 4, hh % 2)
                    V.wait_ge(ldw.h, ldw.n)
                    conv_w(4, hh % 2)
                    for th in range(2):
                        tq = slice(512 * th, 512 * (th + 1))
                        sl = slice(BND + 512 * th, BND + 512 * (th + 1))
                        Te.wait_ge(dv.h, dv.n)
                        for k in range(4):
                            inst = Te.matmul(pq[:, 0, :], lhsT=wtile(hh % 2, k),
                                             rhs=qcT[:, k * TSL + 512 * th:
                                                     k * TSL + 512 * (th + 1)],
                                             start=(k == 0), stop=(k == 3))
                        pe.inc(inst)
                        V.wait_ge(pe.h, pe.n)
                        P = pq[:, 0, :]
                        cs_, sn_ = C["cos2"][:, sl], C["sin2"][:, sl]
                        V.tensor_tensor(out=tmpA[0:32, :], in0=P[0:32, :],
                                        in1=cs_[0:32, :], op=ALU.mult)
                        V.tensor_tensor(out=tmpB[0:32, :], in0=P[32:64, :],
                                        in1=sn_[32:64, :], op=ALU.mult)
                        V.tensor_tensor(out=tmpA[32:64, :], in0=P[32:64, :],
                                        in1=cs_[32:64, :], op=ALU.mult)
                        V.tensor_tensor(out=tmpB[32:64, :], in0=P[0:32, :],
                                        in1=sn_[0:32, :], op=ALU.mult)
                        V.tensor_tensor(out=qr_s[0:32, :], in0=tmpA[0:32, :],
                                        in1=tmpB[0:32, :], op=ALU.subtract)
                        V.tensor_tensor(out=qr_s[32:64, :], in0=tmpB[32:64, :],
                                        in1=tmpA[32:64, :], op=ALU.add)
                        dv.inc(V.tensor_copy(out=qr_s[64:128, :], in_=P[64:128, :]))
                        Sc.wait_ge(dv.h, dv.n)
                        ac.inc(Sc.activation(out=qsq[:], in_=qr_s[:], func=AF.Square,
                                             bias=0.0, scale=1.0))
                        Te.wait_ge(ac.h, ac.n)
                        pe.inc(Te.matmul(psums[:, :], lhsT=C["onescb"][:, :], rhs=qsq[:],
                                         start=True, stop=True))
                        Sc.wait_ge(pe.h, pe.n)
                        ac.inc(Sc.activation(out=rowA[:, 0:512], in_=psums[:, :],
                                             func=AF.Sqrt, bias=C["actc"][0:1, 2:3],
                                             scale=C["actc"][0:1, 3:4]))
                        V.wait_ge(ac.h, ac.n)
                        dv.inc(V.reciprocal(out=rowB[:, 0:512], in_=rowA[:, 0:512]))
                        Te.wait_ge(dv.h, dv.n)
                        pe.inc(Te.matmul(pbc[:, :], lhsT=C["onesrf"][:, :],
                                         rhs=rowB[:, 0:512], start=True, stop=True))
                        V.wait_ge(pe.h, pe.n)
                        dv.inc(V.scalar_tensor_tensor(
                            out=qT[:, hh * TSL + 512 * th: hh * TSL + 512 * (th + 1)],
                            in0=qr_s[:], scalar=qwk[:, 0:1], in1=pbc[:, :],
                            op0=ALU.mult, op1=ALU.mult))

            barrier()
            wstack.close()
            with ExitStack() as b4:
                psums4 = b4.enter_context(nc.psum_tensor("psums4", [1, 512], F32))
                pbc4 = b4.enter_context(nc.psum_tensor("pbc4", [128, 512], F32))
                ptr_b = b4.enter_context(nc.psum_tensor("ptr_b", [128, 512], BF16))
                ptr_f = b4.enter_context(nc.psum_tensor("ptr_f", [128, 512], F32))
                pqi = b4.enter_context(nc.psum_tensor("pqi", [128, 2, 512], F32))
                ksq = b4.enter_context(nc.sbuf_tensor("ksq", [128, NT], BF16))
                knf = projT[:, 0 * NT:1 * NT]
                kA = projT[:, 1 * NT:2 * NT]
                vbf = b4.enter_context(nc.sbuf_tensor("vbf", [128, NT], BF16))
                dqb = b4.enter_context(nc.sbuf_tensor("dqb", [64, TSL], BF16))
                wiub = b4.enter_context(nc.sbuf_tensor("wiub", [64, 256], BF16))

                kraw = projT[:, 2 * NT:3 * NT]
                dv.inc(V.tensor_copy(out=vbf[:, :], in_=projT[:, 3 * NT:4 * NT]))
                kB = projT[:, 3 * NT:4 * NT]
                ac.inc(Sc.activation(out=ksq[:, :], in_=kraw, func=AF.Square,
                                     bias=0.0, scale=1.0))
                for ntile in range(3):
                    ncols = 512 if ntile < 2 else 256
                    Te.wait_ge(ac.h, ac.n)
                    pe.inc(Te.matmul(psums4[:, 0:ncols], lhsT=C["onescb"][:, :],
                                     rhs=ksq[:, 512 * ntile:512 * ntile + ncols],
                                     start=True, stop=True))
                    Sc.wait_ge(pe.h, pe.n)
                    ac.inc(Sc.activation(out=rowA[:, 512 * ntile:512 * ntile + ncols],
                                         in_=psums4[:, 0:ncols], func=AF.Sqrt,
                                         bias=C["actc"][0:1, 2:3], scale=C["actc"][0:1, 3:4]))
                V.wait_ge(ac.h, ac.n)
                dv.inc(V.reciprocal(out=rowB[:, 0:NT], in_=rowA[:, 0:NT]))
                for ntile in range(3):
                    ncols = 512 if ntile < 2 else 256
                    Te.wait_ge(dv.h, dv.n)
                    pe.inc(Te.matmul(pbc4[:, 0:ncols], lhsT=C["onesrf"][:, :],
                                     rhs=rowB[:, 512 * ntile:512 * ntile + ncols],
                                     start=True, stop=True))
                    V.wait_ge(pe.h, pe.n)
                    dv.inc(V.scalar_tensor_tensor(
                        out=knf[:, 512 * ntile:512 * ntile + ncols],
                        in0=kraw[:, 512 * ntile:512 * ntile + ncols],
                        scalar=C["knw"][:, 0:1], in1=pbc4[:, 0:ncols],
                        op0=ALU.mult, op1=ALU.mult))
                V.tensor_tensor(out=kA[0:32, :], in0=knf[0:32, :], in1=C["cos2"][0:32, :],
                                op=ALU.mult)
                V.tensor_tensor(out=kB[0:32, :], in0=knf[32:64, :], in1=C["sin2"][32:64, :],
                                op=ALU.mult)
                V.tensor_tensor(out=kA[32:64, :], in0=knf[32:64, :], in1=C["cos2"][32:64, :],
                                op=ALU.mult)
                V.tensor_tensor(out=kB[32:64, :], in0=knf[0:32, :], in1=C["sin2"][0:32, :],
                                op=ALU.mult)
                V.tensor_tensor(out=kT[0:32, :], in0=kA[0:32, :], in1=kB[0:32, :],
                                op=ALU.subtract)
                V.tensor_tensor(out=kT[32:64, :], in0=kB[32:64, :], in1=kA[32:64, :],
                                op=ALU.add)
                V.tensor_copy(out=kT[64:128, :], in_=knf[64:128, :])
                Te.wait_ge(dv.h, dv.n)
                for i in range(10):
                    par = i % 4
                    if par == 0:
                        if i > 0:
                            Te.wait_ge(dv.h, dv.n)
                    inst = Te.transpose(ptr_b[:, par * 128:(par + 1) * 128],
                                        in_=vbf[:, 128 * i:128 * (i + 1)],
                                        identity=C["idb"][:, :])
                    pe.inc(inst)
                    if par == 3 or i == 9:
                        V.wait_ge(pe.h, pe.n)
                        for u in range(par + 1):
                            ii = i - par + u
                            inst = V.tensor_copy(out=vrows[:, 128 * ii:128 * (ii + 1)],
                                                 in_=ptr_b[:, u * 128:(u + 1) * 128])
                        dv.inc(inst)
                Sc.wait_ge(ldw.h, GATHER_LD)
                V.wait_ge(ldw.h, GATHER_LD)
                dv.inc(V.tensor_copy(out=kcTb[:, :], in_=kcT[:, :]))
                ac.inc(Sc.activation(out=ksq[:, 0:NB], in_=ckvT[:, :], func=AF.Square,
                                     bias=0.0, scale=1.0))
                Te.wait_ge(ac.h, ac.n)
                pe.inc(Te.matmul(psums4[:, :], lhsT=C["onescb"][:, :],
                                 rhs=ksq[:, 0:NB], start=True, stop=True))
                Sc.wait_ge(pe.h, pe.n)
                ac.inc(Sc.activation(out=rowA[:, 0:NB], in_=psums4[:, :], func=AF.Sqrt,
                                     bias=C["actc"][0:1, 2:3], scale=C["actc"][0:1, 3:4]))
                V.wait_ge(ac.h, ac.n)
                dv.inc(V.reciprocal(out=rowB[:, 0:NB], in_=rowA[:, 0:NB]))
                Te.wait_ge(dv.h, dv.n)
                pe.inc(Te.matmul(pbc4[:, :], lhsT=C["onesrf"][:, :], rhs=rowB[:, 0:NB],
                                 start=True, stop=True))
                V.wait_ge(pe.h, pe.n)
                dv.inc(V.scalar_tensor_tensor(out=kvnT[:, :], in0=ckvT[:, :],
                                              scalar=C["knw"][:, 0:1], in1=pbc4[:, :],
                                              op0=ALU.mult, op1=ALU.mult))
                Te.wait_ge(dv.h, dv.n)
                for u in range(4):
                    inst = Te.transpose(ptr_b[:, u * 128:(u + 1) * 128],
                                        in_=kvnT[:, 128 * u:128 * (u + 1)],
                                        identity=C["idb"][:, :])
                pe.inc(inst)
                V.wait_ge(pe.h, pe.n)
                dv.inc(V.tensor_copy(out=kvnr[:, :], in_=ptr_b[:, :]))
                V.tensor_copy(out=wiub[:, :], in_=C["wiuq"][:, :])
                dv.inc(V.tensor_copy(out=dqb[:, :],
                                     in_=projT[64:128, 4 * NT + BND:5 * NT]))
                for mq in range(2):
                    Te.wait_ge(dv.h, dv.n)
                    for nt in range(2):
                        inst = Te.matmul(pqi[:, nt, :],
                                         lhsT=wiub[:, 128 * mq:128 * (mq + 1)],
                                         rhs=dqb[:, 512 * nt:512 * (nt + 1)],
                                         start=True, stop=True)
                    pe.inc(inst)
                    V.wait_ge(pe.h, pe.n)
                    V.tensor_copy(out=qiT[:, (2 * mq) * TSL:(2 * mq + 1) * TSL],
                                  in_=pqi[0:64, :, :].rearrange("p a b -> p (a b)"))
                    dv.inc(V.tensor_copy(out=qiT[:, (2 * mq + 1) * TSL:(2 * mq + 2) * TSL],
                                         in_=pqi[64:128, :, :].rearrange("p a b -> p (a b)")))
                for i in range(8):
                    Te.wait_ge(dv.h, dv.n)
                    pe.inc(Te.transpose(ptr_f[:, 0:128],
                                        in_=projT[:, 5 * NT + BND + 128 * i:
                                                  5 * NT + BND + 128 * (i + 1)],
                                        identity=C["idf"][:, :]))
                    V.wait_ge(pe.h, pe.n)
                    dv.inc(V.tensor_copy(out=wihc[:, 4 * i:4 * (i + 1)],
                                         in_=ptr_f[:, 64:68]))

            tap("projT", projT[:, :])
            tap("qcT", qcT[:, :])
            tap("ckvT", ckvT[:, :])
            tap("kcT", kcT[:, :])
            tap("qiT", qiT[:, :])
            tap("wihc", wihc[:, :])
            barrier()
            with ExitStack() as b5:
                phs = b5.enter_context(nc.psum_tensor("phs", [128, 512], F32))
                ptr_b5 = b5.enter_context(nc.psum_tensor("ptr_b5", [128, 512], BF16))
                isc = b5.enter_context(nc.sbuf_tensor("isc", [128, 2, NB], F32))
                work = b5.enter_context(nc.sbuf_tensor("work", [128, 2, NB], F32))
                seltmp = b5.enter_context(nc.sbuf_tensor("seltmp", [128, 2, NB], F32))
                selm = b5.enter_context(nc.sbuf_tensor("selm", [128, 2, NB], U8))
                s8 = b5.enter_context(nc.sbuf_tensor("s8", [128, 2, 8], F32))
                Btile = b5.enter_context(nc.sbuf_tensor("Btile", [128, 2, NB], BF16))

                for pair in range(4):
                    for u in (0, 1):
                        tile = 2 * pair + u
                        tsl_ = slice(128 * tile, 128 * (tile + 1))
                        for h4 in range(4):
                            Te.wait_ge(dv.h, dv.n)
                            pe.inc(Te.matmul(
                                phs[:, :],
                                lhsT=qiT[:, h4 * TSL + 128 * tile:
                                         h4 * TSL + 128 * (tile + 1)],
                                rhs=kcTb[:, :], start=True, stop=True))
                            V.wait_ge(pe.h, pe.n)
                            if h4 == 0:
                                dv.inc(V.tensor_scalar(
                                    out=isc[:, u, :], in0=phs[:, :], scalar1=0.0,
                                    scalar2=wihc[:, 4 * tile + h4:4 * tile + h4 + 1],
                                    op0=ALU.max, op1=ALU.mult))
                            else:
                                V.tensor_scalar(
                                    out=seltmp[:, u, :], in0=phs[:, :], scalar1=0.0,
                                    scalar2=wihc[:, 4 * tile + h4:4 * tile + h4 + 1],
                                    op0=ALU.max, op1=ALU.mult)
                                vspace()
                                dv.inc(V.tensor_tensor(out=isc[:, u, :], in0=isc[:, u, :],
                                                       in1=seltmp[:, u, :], op=ALU.add))
                    for u in (0, 1):
                        tile = 2 * pair + u
                        V.tensor_tensor(out=work[:, u, :], in0=isc[:, u, :],
                                        in1=C["ladder"][:, :], op=ALU.add)
                        V.tensor_scalar(out=selm[:, u, :], in0=C["niota"][:, :],
                                        scalar1=C["nvcol"][:, tile:tile + 1],
                                        scalar2=None, op0=ALU.is_ge)
                    for u in (0, 1):
                        V.copy_predicated(out=work[:, u, :], mask=selm[:, u, :],
                                          data=C["neglad"][:, :])
                    for r in range(8):
                        for u in (0, 1):
                            V.max(out=s8[:, u, :], in_=work[:, u, :])
                        if r < 7:
                            for u in (0, 1):
                                V.tensor_scalar(out=selm[:, u, :], in0=work[:, u, :],
                                                scalar1=s8[:, u, 7:8], scalar2=None,
                                                op0=ALU.is_ge)
                            for u in (0, 1):
                                V.copy_predicated(out=work[:, u, :], mask=selm[:, u, :],
                                                  data=C["neglad"][:, :])
                    for u in (0, 1):
                        tile = 2 * pair + u
                        V.tensor_scalar(out=seltmp[:, u, :], in0=C["niota"][:, :],
                                        scalar1=C["nvcol"][:, tile:tile + 1],
                                        scalar2=None, op0=ALU.is_lt)
                        V.tensor_tensor(out=work[:, u, :], in0=isc[:, u, :],
                                        in1=C["ladder"][:, :], op=ALU.add)
                    for u in (0, 1):
                        tile = 2 * pair + u
                        V.scalar_tensor_tensor(out=Btile[:, u, :], in0=work[:, u, :],
                                               scalar=s8[:, u, 7:8], in1=seltmp[:, u, :],
                                               op0=ALU.is_ge, op1=ALU.mult)
                    for u in (0, 1):
                        tile = 2 * pair + u
                        dv.inc(V.tensor_scalar(out=Btile[:, u, 0:1], in0=Btile[:, u, 0:1],
                                               scalar1=C["f0col"][:, tile:tile + 1],
                                               scalar2=None, op0=ALU.add))
                    for u in (0, 1):
                        tile = 2 * pair + u
                        Te.wait_ge(dv.h, dv.n)
                        for j in range(4):
                            inst = Te.transpose(ptr_b5[:, j * 128:(j + 1) * 128],
                                                in_=Btile[:, u, 128 * j:128 * (j + 1)],
                                                identity=C["idb"][:, :])
                        pe.inc(inst)
                        V.wait_ge(pe.h, pe.n)
                        for j in range(4):
                            inst = V.tensor_copy(
                                out=BTm[:, j * TSL + 128 * tile: j * TSL + 128 * (tile + 1)],
                                in_=ptr_b5[:, j * 128:(j + 1) * 128])
                        dv.inc(inst)

        tap("xlnT", xlnT[:, :])
        tap("qT", qT[:, :])
        tap("kT", kT[:, :])
        tap("kvnT", kvnT[:, :])
        tap("BTm", BTm[:, :])
        barrier()
        abctx.close()
        attnT = ce0(nc.sbuf_tensor("attnT", [128, H * TSL], BF16))

        with ExitStack() as cctx:
            ce = cctx.enter_context
            psC = ce(nc.psum_tensor("psC", [128, 4 * 512], F32))
            psAV = ce(nc.psum_tensor("psAV", [128, 512], F32))
            psS = ce(nc.psum_tensor("psS", [1, 512], F32))
            psB = ce(nc.psum_tensor("psB", [128, 512], F32))
            e_bf = ce(nc.sbuf_tensor("e_bf", [128, 4 * 512], BF16))
            pB = ce(nc.sbuf_tensor("pB", [128, 4 * 512], BF16))
            rsb = ce(nc.sbuf_tensor("rsb", [128, 512], F32))
            tm_bf = ce(nc.sbuf_tensor("tm_bf", [128, 512], BF16))

            for hh in range(H):
                for th in range(2):
                    tq = slice(512 * th, 512 * (th + 1))
                    Te.wait_ge(ac.h, ac.n)
                    Te.wait_ge(dv.h, dv.n)
                    for j in range(4):
                        inst = Te.matmul(psC[:, j * 512:(j + 1) * 512],
                                         lhsT=kvnT[:, 128 * j:128 * (j + 1)],
                                         rhs=qT[:, hh * TSL + 512 * th:
                                                hh * TSL + 512 * (th + 1)],
                                         start=True, stop=True)
                    pe.inc(inst)
                    Sc.wait_ge(pe.h, pe.n)
                    ac.inc(Sc.activation(out=e_bf[:, :], in_=psC[:, :], func=AF.Exp,
                                         bias=0.0, scale=1.0))
                    V.wait_ge(ac.h, ac.n)
                    dv.inc(V.tensor_tensor(
                        out=pB[:, :].rearrange("p (j f) -> p j f", j=4),
                        in0=e_bf[:, :].rearrange("p (j f) -> p j f", j=4),
                        in1=BTm[:, :].rearrange("p (j f) -> p j f", j=4)[:, :, tq],
                        op=ALU.mult))
                    Te.wait_ge(dv.h, dv.n)
                    for j in range(4):
                        inst = Te.matmul(psS[:, :], lhsT=C["onescb"][:, :],
                                         rhs=pB[:, j * 512:(j + 1) * 512],
                                         start=(j == 0), stop=(j == 3))
                    pe.inc(inst)
                    V.wait_ge(pe.h, pe.n)
                    V.reciprocal(out=rowA[:, 0:512], in_=psS[:, :])
                    vspace()
                    dv.inc(V.tensor_tensor(out=rowB[:, 0:512], in0=rowA[:, 0:512],
                                           in1=C["zrow"][:, tq], op=ALU.mult))
                    Te.wait_ge(dv.h, dv.n)
                    pe.inc(Te.matmul(psB[:, :], lhsT=C["onesrf"][:, :],
                                     rhs=rowB[:, 0:512], start=True, stop=True))
                    for j in range(4):
                        inst = Te.matmul(psAV[:, :], lhsT=kvnr[:, 128 * j:128 * (j + 1)],
                                         rhs=pB[:, j * 512:(j + 1) * 512],
                                         start=(j == 0), stop=(j == 3))
                    pe.inc(inst)
                    V.wait_ge(pe.h, pe.n)
                    V.tensor_copy(out=rsb[:, :], in_=psB[:, :])
                    vspace()
                    dv.inc(V.scalar_tensor_tensor(
                        out=attnT[:, hh * TSL + 512 * th: hh * TSL + 512 * (th + 1)],
                        in0=psAV[:, :], scalar=1.0, in1=rsb[:, :],
                        op0=ALU.mult, op1=ALU.mult))

            for hh in range(H):
                for wb in range(4):
                    tq = slice(WIN * wb, WIN * (wb + 1))
                    Te.wait_ge(ac.h, ac.n)
                    Te.wait_ge(dv.h, dv.n)
                    for j in range(4):
                        inst = Te.matmul(psC[:, j * WIN:(j + 1) * WIN],
                                         lhsT=kT[:, WIN * wb + 128 * j:
                                                 WIN * wb + 128 * (j + 1)],
                                         rhs=qT[:, hh * TSL + WIN * wb:
                                                hh * TSL + WIN * (wb + 1)],
                                         start=True, stop=True)
                    pe.inc(inst)
                    Sc.wait_ge(pe.h, pe.n)
                    ac.inc(Sc.activation(out=e_bf[:, 0:4 * WIN], in_=psC[:, 0:4 * WIN],
                                         func=AF.Exp, bias=0.0, scale=1.0))
                    V.wait_ge(ac.h, ac.n)
                    msk = C["bw0"] if wb == 0 else C["bwg"]
                    dv.inc(V.tensor_tensor(out=pB[:, 0:4 * WIN], in0=e_bf[:, 0:4 * WIN],
                                           in1=msk[:, :], op=ALU.mult))
                    Te.wait_ge(dv.h, dv.n)
                    for j in range(4):
                        inst = Te.matmul(psS[:, 0:WIN], lhsT=C["onescb"][:, :],
                                         rhs=pB[:, j * WIN:(j + 1) * WIN],
                                         start=(j == 0), stop=(j == 3))
                    pe.inc(inst)
                    V.wait_ge(pe.h, pe.n)
                    dv.inc(V.reciprocal(out=rowB[:, 0:WIN], in_=psS[:, 0:WIN]))
                    Te.wait_ge(dv.h, dv.n)
                    pe.inc(Te.matmul(psB[:, 0:WIN], lhsT=C["onesrf"][:, :],
                                     rhs=rowB[:, 0:WIN], start=True, stop=True))
                    for j in range(4):
                        inst = Te.matmul(psAV[:, 0:WIN],
                                         lhsT=vrows[:, (2 * wb + j) * 128:
                                                    (2 * wb + j + 1) * 128],
                                         rhs=pB[:, j * WIN:(j + 1) * WIN],
                                         start=(j == 0), stop=(j == 3))
                    pe.inc(inst)
                    V.wait_ge(pe.h, pe.n)
                    V.tensor_copy(out=rsb[:, 0:WIN], in_=psB[:, 0:WIN])
                    vspace()
                    V.scalar_tensor_tensor(out=tm_bf[:, 0:WIN], in0=psAV[:, 0:WIN],
                                           scalar=1.0, in1=rsb[:, 0:WIN],
                                           op0=ALU.mult, op1=ALU.mult)
                    vspace()
                    dst = attnT[:, hh * TSL + WIN * wb: hh * TSL + WIN * (wb + 1)]
                    dv.inc(V.tensor_tensor(out=dst, in0=dst, in1=tm_bf[:, 0:WIN],
                                           op=ALU.add))

        tap("attnT", attnT[:, :])
        barrier()
        e1ctx = ExitStack()
        grctx = ExitStack()
        if True:
            e1 = e1ctx.enter_context
            goT = grctx.enter_context(nc.sbuf_tensor("goT", [128, H * TSL], BF16, side="right"))
            with ExitStack() as e1a:
                wsb2 = e1a.enter_context(nc.sbuf_tensor("wsb2", [128, 2, 2048], F32))
                wsb2_16 = e1a.enter_context(nc.sbuf_tensor("wsb2_16", [128, 2, 2048], BF16))
                psE = e1a.enter_context(nc.psum_tensor("psE", [128, 2, 512], F32))

                def stream2(name, mi, nk, parity, extra=None):
                    src = ap[name]
                    if extra is not None:
                        src = src[extra]
                    src = src.rearrange("m p k c -> m p (k c)")[mi]
                    Sy.wait_ge(pe.h, pe.n)
                    ldw.inc(Sy.dma_start(out=wsb2[:, parity, 0:nk * 128], in_=src), 16)

                def w2tile(parity, k):
                    return wsb2_16[:, parity, 128 * k:128 * (k + 1)]

                def conv2(nk, parity):
                    dv.inc(V.tensor_copy(out=wsb2_16[:, parity, 0:nk * 128],
                                         in_=wsb2[:, parity, 0:nk * 128]))

                idx = 0
                for g in range(4):
                    for mo in range(4):
                        stream2("wgroup", mo, 4, idx % 2, extra=g)
                        V.wait_ge(ldw.h, ldw.n)
                        conv2(4, idx % 2)
                        Te.wait_ge(dv.h, dv.n)
                        for nt in range(2):
                            for k in range(4):
                                inst = Te.matmul(psE[:, nt, :], lhsT=w2tile(idx % 2, k),
                                                 rhs=attnT[:, (4 * g + k) * TSL + 512 * nt:
                                                           (4 * g + k) * TSL + 512 * (nt + 1)],
                                                 start=(k == 0), stop=(k == 3))
                        pe.inc(inst)
                        V.wait_ge(pe.h, pe.n)
                        dv.inc(V.tensor_copy(
                            out=goT[:, (4 * g + mo) * TSL:(4 * g + mo + 1) * TSL],
                            in_=psE[:, :, :].rearrange("p a b -> p (a b)")))
                        idx += 1

            barrier()
            cdctx.close()

            x1sb = e1(nc.sbuf_tensor("x1sb", [128, 8, HID], F32))
            xv8 = ap["x"].rearrange("(i p) h -> i p h", p=128)
            for i in range(8):
                ldx.inc(Sy.dma_start(out=x1sb[:, i, :], in_=xv8[i + 2]), 16)
            X1_LD = ldx.n

            barrier()
            with ExitStack() as e2a:
                wsb3 = e2a.enter_context(nc.sbuf_tensor("wsb3", [128, 2, 2048], F32))
                wsb3_16 = e2a.enter_context(nc.sbuf_tensor("wsb3_16", [128, 2, 2048], BF16))
                psE2 = e2a.enter_context(nc.psum_tensor("psE2", [128, 2, 512], F32))
                psT = e2a.enter_context(nc.psum_tensor("psT", [128, 512], F32))
                aof = e2a.enter_context(nc.sbuf_tensor("aof", [128, 1024], F32))

                def w3tile(parity, k):
                    return wsb3_16[:, parity, 128 * k:128 * (k + 1)]

                V.wait_ge(ldx.h, X1_LD)
                for mo in range(16):
                    Sy.wait_ge(pe.h, pe.n)
                    src = ap["wfinal"].rearrange("m p k c -> m p (k c)")[mo]
                    ldw.inc(Sy.dma_start(out=wsb3[:, mo % 2, :], in_=src), 16)
                    V.wait_ge(ldw.h, ldw.n)
                    dv.inc(V.tensor_copy(out=wsb3_16[:, mo % 2, :],
                                         in_=wsb3[:, mo % 2, :]))
                    Te.wait_ge(dv.h, dv.n)
                    for nt in range(2):
                        for k in range(16):
                            inst = Te.matmul(psE2[:, nt, :], lhsT=w3tile(mo % 2, k),
                                             rhs=goT[:, k * TSL + 512 * nt:
                                                     k * TSL + 512 * (nt + 1)],
                                             start=(k == 0), stop=(k == 15))
                    pe.inc(inst)
                    V.wait_ge(pe.h, pe.n)
                    dv.inc(V.tensor_copy(out=aof[:, :],
                                         in_=psE2[:, :, :].rearrange("p a b -> p (a b)")))
                    for tt in range(8):
                        Te.wait_ge(dv.h, dv.n)
                        pe.inc(Te.transpose(psT[:, 0:128],
                                            in_=aof[:, 128 * tt:128 * (tt + 1)],
                                            identity=C["idf"][:, :]))
                        V.wait_ge(pe.h, pe.n)
                        dst = x1sb[:, tt, 128 * mo:128 * (mo + 1)]
                        dv.inc(V.tensor_tensor(out=dst, in0=dst, in1=psT[:, 0:128],
                                               op=ALU.add))

        tap("x1sb", x1sb[:, :, :].rearrange("p a b -> p (a b)"))
        barrier()
        grctx.close()
        wait_all_compute(Sy)
        x1v = x1d.rearrange("(i p) h -> i p h", p=128)
        for i in range(8):
            st.inc(Sy.dma_start(out=x1v[i], in_=x1sb[:, i, :]), 16)
        X1_ST = st.n

        barrier()
        xrctx = ExitStack()
        xln2T = xrctx.enter_context(nc.sbuf_tensor("xln2T", [128, 16 * TSL], BF16, side="right"))
        with ExitStack() as fctx:
            fe = fctx.enter_context
            sc2 = fe(nc.sbuf_tensor("sc2", [128, 2, HID], F32))
            st2 = fe(nc.sbuf_tensor("st2", [128, 2, 8], F32))
            xl2n = fe(nc.sbuf_tensor("xl2n", [128, 2, HID], BF16))
            ptr_c = fe(nc.psum_tensor("ptr_c", [128, 512], BF16))
            for i in range(8):
                par = i % 2
                xi = x1sb[:, i, :]
                S = lambda k: st2[:, par, k:k + 1]
                V.tensor_reduce(out=S(0), in_=xi, axis=AX.X, op=ALU.add)
                V.scalar_tensor_tensor(out=sc2[:, par, :], in0=xi, scalar=1.0, in1=xi,
                                       op0=ALU.mult, op1=ALU.mult, accum_out=S(1))
                V.tensor_scalar(out=S(2), in0=S(0), scalar1=1.0 / HID, scalar2=None,
                                op0=ALU.mult)
                V.tensor_scalar(out=S(3), in0=S(1), scalar1=1.0 / HID, scalar2=None,
                                op0=ALU.mult)
                dv.inc(V.scalar_tensor_tensor(out=S(4), in0=S(2), scalar=S(2), in1=S(3),
                                               op0=ALU.mult, op1=ALU.subtract))
                Sc.wait_ge(dv.h, dv.n)
                ac.inc(Sc.activation(out=S(6), in_=S(4), func=AF.Sqrt,
                                     bias=C["actc"][:, 0:1], scale=C["actc"][:, 1:2]))
                V.wait_ge(ac.h, ac.n)
                V.reciprocal(out=S(7), in_=S(6))
                vspace()
                dv.inc(V.tensor_scalar(out=xl2n[:, par, :], in0=xi, scalar1=S(2),
                                       scalar2=S(7), op0=ALU.subtract, op1=ALU.mult))
                for cb in range(4):
                    Te.wait_ge(dv.h, dv.n)
                    for u in range(4):
                        hck = cb * 4 + u
                        inst = Te.transpose(ptr_c[:, u * 128:(u + 1) * 128],
                                            in_=xl2n[:, par, 128 * hck:128 * (hck + 1)],
                                            identity=C["idb"][:, :])
                    pe.inc(inst)
                    V.wait_ge(pe.h, pe.n)
                    for u in range(4):
                        hck = cb * 4 + u
                        inst = V.tensor_copy(
                            out=xln2T[:, hck * TSL + 128 * i: hck * TSL + 128 * (i + 1)],
                            in_=ptr_c[:, u * 128:(u + 1) * 128])
                    dv.inc(inst)

        barrier()
        e1ctx.close()
        hc = e(nc.sbuf_tensor("hc", [128, 64 * TSL], BF16))
        with ExitStack() as gctx:
            ge = gctx.enter_context
            wgu = ge(nc.sbuf_tensor("wgu", [128, 2, 2, 2048], U16))
            wgu16 = wgu[:, :, :, :].bitcast(BF16)
            psG = ge(nc.psum_tensor("psG", [128, 2, 512], F32))
            psU = ge(nc.psum_tensor("psU", [128, 2, 512], F32))
            sg = ge(nc.sbuf_tensor("sg", [128, 1024], F32))

            def gutile(parity, m, k):
                return wgu16[:, parity, m, 128 * k:128 * (k + 1)]


            wgv = ap["wg_t"].rearrange("m p k c -> m p (k c)")
            wuv = ap["wu_t"].rearrange("m p k c -> m p (k c)")
            ldf.inc(Gp.dma_start(out=wgu[:, 0, 0, :], in_=wgv[0]), 16)
            ldf.inc(Gp.dma_start(out=wgu[:, 0, 1, :], in_=wuv[0]), 16)
            for ic in range(64):
                par = ic % 2
                if ic + 1 < 64:
                    Gp.wait_ge(pe.h, pe.n)
                    ldf.inc(Gp.dma_start(out=wgu[:, 1 - par, 0, :], in_=wgv[ic + 1]), 16)
                    ldf.inc(Gp.dma_start(out=wgu[:, 1 - par, 1, :], in_=wuv[ic + 1]), 16)
                V.wait_ge(ldf.h, 32 * (ic + 1))
                for m in (0, 1):
                    dv.inc(V.tensor_scalar(out=wgu16[:, par, m, 0:2048], in0=wgu[:, par, m, :],
                                           scalar1=-128.0, scalar2=None, op0=ALU.add))
                Te.wait_ge(dv.h, dv.n)
                if ic > 0:
                    Te.wait_ge(dv.h, dv.n)
                for nt in range(2):
                    for k in range(16):
                        Te.matmul(psG[:, nt, :], lhsT=gutile(par, 0, k),
                                  rhs=xln2T[:, k * TSL + 512 * nt:
                                            k * TSL + 512 * (nt + 1)],
                                  start=(k == 0), stop=(k == 15))
                for nt in range(2):
                    for k in range(16):
                        inst = Te.matmul(psU[:, nt, :], lhsT=gutile(par, 1, k),
                                         rhs=xln2T[:, k * TSL + 512 * nt:
                                                   k * TSL + 512 * (nt + 1)],
                                         start=(k == 0), stop=(k == 15))
                pe.inc(inst)
                Sc.wait_ge(pe.h, pe.n)
                ac.inc(Sc.activation(out=sg[:, :],
                                     in_=psG[:, :, :].rearrange("p a b -> p (a b)"),
                                     func=AF.Silu, bias=0.0,
                                     scale=C["gsc"][:, ic:ic + 1]))
                V.wait_ge(ac.h, ac.n)
                dv.inc(V.scalar_tensor_tensor(
                    out=hc[:, ic * TSL:(ic + 1) * TSL],
                    in0=psU[:, :, :].rearrange("p a b -> p (a b)"),
                    scalar=C["usc"][:, ic:ic + 1], in1=sg[:, :],
                    op0=ALU.mult, op1=ALU.mult))

        barrier()
        xrctx.close()
        with ExitStack() as hctx:
            he = hctx.enter_context
            wdb = he(nc.sbuf_tensor("wdb", [128, 2, 2048], U16))
            wdb16 = wdb[:, :, :].bitcast(BF16)
            psD = he(nc.psum_tensor("psD", [128, 2, 512], F32))
            psT2 = he(nc.psum_tensor("psT2", [128, 512], F32))
            dfs = he(nc.sbuf_tensor("dfs", [128, 512], F32))
            x1tb = he(nc.sbuf_tensor("x1tb", [128, 4, HID], F32))

            wdv = ap["wd_t"].rearrange("m p k c -> m p k c")
            for tb in range(2):
                Sy.wait_ge(st.h, X1_ST)
                Sy.wait_ge(dv.h, dv.n)
                for u in range(4):
                    ldx.inc(Sy.dma_start(out=x1tb[:, u, :], in_=x1v[4 * tb + u]), 16)
                V.wait_ge(ldx.h, ldx.n)
                for ho in range(16):
                    for sub in range(4):
                        par = sub % 2
                        src = wdv[ho][:, 16 * sub:16 * (sub + 1), :] \
                            .rearrange("p k c -> p (k c)")
                        Gp.wait_ge(pe.h, pe.n)
                        ldf.inc(Gp.dma_start(out=wdb[:, par, :], in_=src), 16)
                        V.wait_ge(ldf.h, ldf.n)
                        dv.inc(V.tensor_scalar(out=wdb16[:, par, 0:2048], in0=wdb[:, par, :],
                                               scalar1=-128.0, scalar2=None, op0=ALU.add))
                        Te.wait_ge(dv.h, dv.n)
                        for k in range(16):
                            ig = 16 * sub + k
                            inst = Te.matmul(
                                psD[:, 0, :],
                                lhsT=wdb16[:, par, 128 * k:128 * (k + 1)],
                                rhs=hc[:, ig * TSL + 512 * tb: ig * TSL + 512 * (tb + 1)],
                                start=(ig == 0), stop=(ig == 63))
                        pe.inc(inst)
                    V.wait_ge(pe.h, pe.n)
                    dv.inc(V.tensor_scalar(out=dfs[:, :], in0=psD[:, 0, :],
                                           scalar1=C["dsc"][:, ho:ho + 1],
                                           scalar2=None, op0=ALU.mult))
                    for u in range(4):
                        Te.wait_ge(dv.h, dv.n)
                        pe.inc(Te.transpose(psT2[:, 0:128],
                                            in_=dfs[:, 128 * u:128 * (u + 1)],
                                            identity=C["idf"][:, :]))
                        V.wait_ge(pe.h, pe.n)
                        dst = x1tb[:, u, 128 * ho:128 * (ho + 1)]
                        dv.inc(V.tensor_tensor(out=dst, in0=dst, in1=psT2[:, 0:128],
                                               op=ALU.add))
                Sy.wait_ge(dv.h, dv.n)
                ov = out.rearrange("(i p) h -> i p h", p=128)
                for u in range(4):
                    st.inc(Sy.dma_start(out=ov[4 * tb + u], in_=x1tb[:, u, :]), 16)

        Sy.wait_ge(st.h, st.n)

    return nc


_NC_CACHE = {}


def kernel(**inputs):
    in_maps = _host_inputs(inputs)
    if "nc" not in _NC_CACHE:
        _NC_CACHE["nc"] = build_nc()
    nc = _NC_CACHE["nc"]
    res = run_bass_kernel_spmd(nc, in_maps, core_ids=list(range(NCORES)))
    o = np.concatenate([res.results[c]["out"] for c in range(NCORES)], 0)
    return o.reshape(1, T, HID).astype(np.float32)


# version 09 (final)
# speedup vs baseline: 1.2452x; 1.2452x over previous
import numpy as np
import ml_dtypes
import concourse.bass as bass
import concourse.mybir as mybir
from concourse.bass_utils import run_bass_kernel_spmd
from contextlib import ExitStack

ALU = mybir.AluOpType
AX = mybir.AxisListType
AF = mybir.ActivationFunctionType
F32 = mybir.dt.float32
BF16 = mybir.dt.bfloat16
U16 = mybir.dt.uint16
U8 = mybir.dt.uint8
NPBF16 = ml_dtypes.bfloat16

T, HID = 8192, 2048
H, HD = 16, 128
M = 16
NB = 512
WIN = 256
NCORES = 8
TSL = 1024
BND = 256
NT = BND + TSL
EPS = 1e-6
LNEPS = 1e-5
EPS_LAD = 1e-30
SCALE = float(np.float32(1.0) / np.sqrt(np.float64(HD)).astype(np.float32))

NO_COLLECTIVE = False
TAPS = []


def _tile_lhst(w):
    K, Mo = w.shape
    return np.ascontiguousarray(
        w.reshape(K // 128, 128, Mo // 128, 128).transpose(2, 1, 0, 3))


def _rope_tables():
    import jax
    import jax.numpy as jnp
    cpu = jax.devices("cpu")[0]
    with jax.default_device(cpu):
        inv = 1.0 / 10000.0 ** (jnp.arange(0, 64, 2, dtype=jnp.float32) / 64)
        ang = jnp.arange(T, dtype=jnp.float32)[:, None] * inv
        return np.asarray(jnp.cos(ang)), np.asarray(jnp.sin(ang))


def _host_inputs(inputs):
    f32 = np.float32
    x = np.asarray(inputs["L0_attn_h"]).reshape(T, HID).astype(f32)

    wsmall = np.zeros((HID, 768), f32)
    wsmall[:, 0:128] = inputs["L0_attn_kvc_w"]
    wsmall[:, 128:256] = inputs["L0_attn_kvc_wz"]
    wsmall[:, 256:384] = inputs["L0_attn_k_proj_w"]
    wsmall[:, 384:512] = inputs["L0_attn_v_proj_w"]
    wsmall[:, 512:576] = inputs["L0_attn_idx_c_w"]
    wsmall[:, 576:640] = inputs["L0_attn_w_dq"]
    wsmall[:, 640:704] = inputs["L0_attn_idx_c_wz"]
    wsmall[:, 704:708] = inputs["L0_attn_w_w"]

    shared = {
        "wsmall": _tile_lhst(wsmall),
        "wqc": _tile_lhst(np.asarray(inputs["L0_attn_w_qc"], f32)),
        "wqup": _tile_lhst(np.asarray(inputs["L0_attn_w_qup"], f32)),
        "wiuq": np.ascontiguousarray(np.asarray(inputs["L0_attn_w_iuq"], f32)),
        "wgroup": np.stack([_tile_lhst(np.asarray(inputs["L0_attn_group_w"], f32)[g])
                            for g in range(4)]),
        "wfinal": _tile_lhst(np.asarray(inputs["L0_attn_final_w"], f32)),
        "wg_t": _tile_lhst(np.asarray(inputs["L0_ffn_gate_w"]).astype(np.uint16).T),
        "wu_t": _tile_lhst(np.asarray(inputs["L0_ffn_up_w"]).astype(np.uint16).T),
        "wd_t": _tile_lhst(np.asarray(inputs["L0_ffn_down_w"]).astype(np.uint16).T),
        "gsc": np.ascontiguousarray(np.asarray(inputs["L0_ffn_gate_scale"], f32).reshape(64, 128).T),
        "usc": np.ascontiguousarray(np.asarray(inputs["L0_ffn_up_scale"], f32).reshape(64, 128).T),
        "dsc": np.ascontiguousarray(np.asarray(inputs["L0_ffn_down_scale"], f32).reshape(16, 128).T),
        "qnw": np.asarray(inputs["L0_attn_q_norm_w"], f32).reshape(128, 1),
        "knw": np.asarray(inputs["L0_attn_k_norm_w"], f32).reshape(128, 1),
        "kvcb": np.ascontiguousarray(np.asarray(inputs["L0_attn_kvc_bias"], f32).T),
        "idxb": np.ascontiguousarray(np.asarray(inputs["L0_attn_idx_c_bias"], f32).T),
    }

    actc = np.zeros((128, 8), f32)
    actc[:, 0] = LNEPS; actc[:, 1] = -1.0; actc[:, 2] = EPS; actc[:, 3] = 1.0 / HD
    shared["actc"] = actc
    shared["idb"] = np.eye(128, dtype=NPBF16)
    shared["idf"] = np.eye(128, dtype=f32)
    shared["onescb"] = np.ones((128, 1), NPBF16)
    shared["onesrf"] = np.ones((1, 128), f32)
    shared["niota"] = np.broadcast_to(np.arange(NB, dtype=f32), (128, NB)).copy()
    lad = (EPS_LAD * (NB - 1 - np.arange(NB, dtype=np.float64))).astype(f32)
    shared["ladder"] = np.broadcast_to(lad, (128, NB)).copy()
    shared["neglad"] = np.broadcast_to((np.float32(-1e30) + lad).astype(f32), (128, NB)).copy()

    i = np.arange(WIN)[:, None]
    j = np.arange(2 * WIN)[None, :]
    mgen = ((j - i > 0) & (j - i <= WIN))
    m0 = (mgen & (j >= WIN))

    def packmask(mk):
        mt = mk.T.astype(NPBF16)
        return np.ascontiguousarray(mt.reshape(4, 128, WIN).transpose(1, 0, 2))

    shared["bwg"] = packmask(mgen)
    bw0 = packmask(m0)

    cosT, sinT = _rope_tables()

    in_maps = []
    for c in range(NCORES):
        t0 = c * TSL
        xs = np.zeros((NT, HID), f32)
        cs = np.zeros((NT, 32), f32)
        sn = np.zeros((NT, 32), f32)
        lo = t0 - BND
        if lo >= 0:
            xs[:] = x[lo:t0 + TSL]
            cs[:] = cosT[lo:t0 + TSL]
            sn[:] = sinT[lo:t0 + TSL]
        else:
            xs[BND:] = x[t0:t0 + TSL]
            cs[BND:] = cosT[t0:t0 + TSL]
            sn[BND:] = sinT[t0:t0 + TSL]

        tg = np.arange(t0, t0 + TSL)
        nv = np.where(tg >= M, tg // M, 0).astype(f32)

        m = dict(shared)
        m["x"] = xs
        m["cos2"] = np.ascontiguousarray(np.concatenate([cs.T, cs.T], 0))
        m["sin2"] = np.ascontiguousarray(np.concatenate([sn.T, sn.T], 0))
        m["nvcol"] = np.ascontiguousarray(nv.reshape(8, 128).T)
        m["zrow"] = (nv > 0).astype(f32).reshape(1, TSL)
        m["f0col"] = np.ascontiguousarray((nv == 0).astype(f32).reshape(8, 128).T)
        m["bw0"] = bw0 if c == 0 else shared["bwg"]
        in_maps.append(m)
    return in_maps


def build_nc(debug_taps=()):
    nc = bass.Bass("TRN2", target_bir_lowering=False, detect_race_conditions=False)
    ap = {}

    def din(name, shape, dtype=F32):
        ap[name] = nc.dram_tensor(name, list(shape), dtype, kind="ExternalInput").ap()

    din("x", (NT, HID))
    din("wsmall", (6, 128, 16, 128)); din("wqc", (4, 128, 16, 128))
    din("wqup", (16, 128, 4, 128)); din("wiuq", (64, 256))
    din("wgroup", (4, 4, 128, 4, 128)); din("wfinal", (16, 128, 16, 128))
    din("wg_t", (64, 128, 16, 128), U16); din("wu_t", (64, 128, 16, 128), U16)
    din("wd_t", (16, 128, 64, 128), U16)
    din("gsc", (128, 64)); din("usc", (128, 64)); din("dsc", (128, 16))
    din("qnw", (128, 1)); din("knw", (128, 1))
    din("kvcb", (128, 16)); din("idxb", (64, 16))
    din("idf", (128, 128)); din("idb", (128, 128), BF16)
    din("onescb", (128, 1), BF16); din("onesrf", (1, 128))
    din("niota", (128, NB)); din("ladder", (128, NB)); din("neglad", (128, NB))
    din("bwg", (128, 4, WIN), BF16); din("bw0", (128, 4, WIN), BF16)
    din("cos2", (64, NT)); din("sin2", (64, NT))
    din("nvcol", (128, 8)); din("zrow", (1, TSL)); din("f0col", (128, 8))
    din("actc", (128, 8))

    out = nc.dram_tensor("out", [TSL, HID], F32, kind="ExternalOutput").ap()
    x1d = nc.dram_tensor("x1scratch", [TSL, HID], F32, kind="Internal").ap()
    cc_in = nc.dram_tensor("cc_in", [128, 128], F32, kind="Internal").ap()
    cc_out = nc.dram_tensor("cc_out", [1024, 128], F32, kind="Internal").ap()
    tap_outs = {}
    for tname, tshape, tdt in debug_taps:
        tap_outs[tname] = nc.dram_tensor("tap_" + tname, list(tshape), tdt,
                                         kind="ExternalOutput").ap()

    V, Sc, Te, Sy, Gp = nc.vector, nc.scalar, nc.tensor, nc.sync, nc.gpsimd

    ctx = ExitStack()
    with ctx:
        e = ctx.enter_context

        class Sem:
            def __init__(self, name):
                self.h = e(nc.semaphore(name)); self.n = 0
            def inc(self, inst, k=1):
                inst.then_inc(self.h, k); self.n += k

        ldx = Sem("s_ldx"); ldw = Sem("s_ldw"); ldf = Sem("s_ldf"); ldv = Sem("s_ldv")
        pe = Sem("s_pe"); dv = Sem("s_dv"); ac = Sem("s_ac"); gp = Sem("s_gp")
        st = Sem("s_st")
        blk = e(nc.Block())

        def tap(name, src_ap):
            if name in tap_outs:
                Sy.wait_ge(dv.h, dv.n); Sy.wait_ge(ac.h, ac.n); Sy.wait_ge(pe.h, pe.n)
                st.inc(Sy.dma_start(out=tap_outs[name], in_=src_ap), 16)

        C = {}
        cshapes = [("qnw", (128, 1), F32), ("knw", (128, 1), F32),
                   ("kvcb", (128, 16), F32), ("idxb", (64, 16), F32),
                   ("idf", (128, 128), F32), ("idb", (128, 128), BF16),
                   ("onescb", (128, 1), BF16), ("onesrf", (1, 128), F32),
                   ("nvcol", (128, 8), F32), ("zrow", (1, TSL), F32),
                   ("f0col", (128, 8), F32), ("gsc", (128, 64), F32),
                   ("usc", (128, 64), F32), ("dsc", (128, 16), F32),
                   ("bwg", (128, 4 * WIN), BF16), ("bw0", (128, 4 * WIN), BF16),
                   ("wiuq", (64, 256), F32),
                   ("actc", (128, 8), F32)]
        for nm, shp, dt in cshapes:
            C[nm] = e(nc.sbuf_tensor("c_" + nm, [shp[0], int(np.prod(shp[1:]))], dt))
        qwk = e(nc.sbuf_tensor("qwk", [128, 1], F32))
        junk = e(nc.sbuf_tensor("junk", [128, 1], F32))

        cdctx = ExitStack()
        ce0 = cdctx.enter_context
        qT = ce0(nc.sbuf_tensor("qT", [128, H * TSL], BF16))
        kT = ce0(nc.sbuf_tensor("kT", [128, NT], BF16))
        vrows = ce0(nc.sbuf_tensor("vrows", [128, 10 * 128], BF16))
        kvnT = ce0(nc.sbuf_tensor("kvnT", [128, NB], BF16))
        kvnr = ce0(nc.sbuf_tensor("kvnr", [128, 4 * 128], BF16))
        BTm = ce0(nc.sbuf_tensor("BTm", [128, 4 * TSL], BF16))
        rowA = ce0(nc.sbuf_tensor("rowA", [1, 1280], F32))
        rowB = ce0(nc.sbuf_tensor("rowB", [1, 1280], F32))

        abctx = ExitStack()
        abe = abctx.enter_context
        cshapes2 = [("niota", (128, NB), F32), ("ladder", (128, NB), F32),
                    ("neglad", (128, NB), F32), ("cos2", (64, NT), F32),
                    ("sin2", (64, NT), F32)]
        for nm, shp, dt in cshapes2:
            C[nm] = abe(nc.sbuf_tensor("c_" + nm, [shp[0], int(np.prod(shp[1:]))], dt))
        for nm, shp, dt in cshapes + cshapes2:
            src = ap[nm]
            if len(src.shape) == 3:
                src = src.rearrange("a b c -> a (b c)")
            ldv.inc(Sc.dma_start(out=C[nm][:, :], in_=src), 16)

        V.wait_ge(ldv.h, ldv.n)
        Sc.wait_ge(ldv.h, ldv.n)
        Te.wait_ge(ldv.h, ldv.n)
        V.tensor_scalar(out=qwk[:], in0=C["qnw"][:], scalar1=SCALE, scalar2=None,
                        op0=ALU.mult)

        def wait_all_compute(eng):
            eng.wait_ge(dv.h, dv.n); eng.wait_ge(ac.h, ac.n); eng.wait_ge(pe.h, pe.n)

        def vspace():
            V.tensor_copy(out=junk[:], in_=qwk[:])

        def barrier():
            for eng in (V, Sc, Te, Sy, Gp):
                wait_all_compute(eng)
                for sm in (st, ldx, ldw, ldf, ldv, gp):
                    eng.wait_ge(sm.h, sm.n)

        xv = ap["x"].rearrange("(i p) h -> i p h", p=128)

        xlnT = abe(nc.sbuf_tensor("xlnT", [128, 16 * NT], BF16))

        with ExitStack() as actx:
            ae = actx.enter_context
            xa = ae(nc.sbuf_tensor("xa", [128, 2, HID], F32))
            xlnn = ae(nc.sbuf_tensor("xlnn", [128, 2, HID], BF16))
            sc1 = ae(nc.sbuf_tensor("sc1", [128, 2, HID], F32))
            stat = ae(nc.sbuf_tensor("stat", [128, 2, 8], F32))
            ptr_a = ae(nc.psum_tensor("ptr_a", [128, 512], BF16))

            for i in range(10):
                par = i % 2
                Sy.wait_ge(dv.h, dv.n)
                ldx.inc(Sy.dma_start(out=xa[:, par, :], in_=xv[i]), 16)
                V.wait_ge(ldx.h, 16 * (i + 1))
                xi = xa[:, par, :]
                S = lambda k: stat[:, par, k:k + 1]
                V.tensor_reduce(out=S(0), in_=xi, axis=AX.X, op=ALU.add)
                V.scalar_tensor_tensor(out=sc1[:, par, :], in0=xi, scalar=1.0, in1=xi,
                                       op0=ALU.mult, op1=ALU.mult, accum_out=S(1))
                V.tensor_scalar(out=S(2), in0=S(0), scalar1=1.0 / HID, scalar2=None,
                                op0=ALU.mult)
                V.tensor_scalar(out=S(3), in0=S(1), scalar1=1.0 / HID, scalar2=None,
                                op0=ALU.mult)
                dv.inc(V.scalar_tensor_tensor(out=S(4), in0=S(2), scalar=S(2), in1=S(3),
                                               op0=ALU.mult, op1=ALU.subtract))
                Sc.wait_ge(dv.h, dv.n)
                ac.inc(Sc.activation(out=S(6), in_=S(4), func=AF.Sqrt,
                                     bias=C["actc"][:, 0:1], scale=C["actc"][:, 1:2]))
                V.wait_ge(ac.h, ac.n)
                V.reciprocal(out=S(7), in_=S(6))
                vspace()
                dv.inc(V.tensor_scalar(out=xlnn[:, par, :], in0=xi, scalar1=S(2),
                                       scalar2=S(7), op0=ALU.subtract, op1=ALU.mult))
                for cb in range(4):
                    Te.wait_ge(dv.h, dv.n)
                    if cb > 0 or i > 0:
                        Te.wait_ge(dv.h, dv.n)
                    for u in range(4):
                        hck = cb * 4 + u
                        inst = Te.transpose(ptr_a[:, u * 128:(u + 1) * 128],
                                            in_=xlnn[:, par, 128 * hck:128 * (hck + 1)],
                                            identity=C["idb"][:, :])
                    pe.inc(inst)
                    V.wait_ge(pe.h, pe.n)
                    for u in range(4):
                        hck = cb * 4 + u
                        inst = V.tensor_copy(
                            out=xlnT[:, hck * NT + 128 * i: hck * NT + 128 * (i + 1)],
                            in_=ptr_a[:, u * 128:(u + 1) * 128])
                    dv.inc(inst)

        barrier()
        barrier()
        with ExitStack() as bctx:
            be = bctx.enter_context
            projT = be(nc.sbuf_tensor("projT", [128, 6 * NT], F32))
            kcT = be(nc.sbuf_tensor("kcT", [64, NB], F32))
            kcTb = be(nc.sbuf_tensor("kcTb", [64, NB], BF16))
            ckvT = be(nc.sbuf_tensor("ckvT", [128, NB], F32))
            wstack = ExitStack()
            wsb = wstack.enter_context(nc.sbuf_tensor("wsb", [128, 2, 2048], F32, side="right"))
            wsb16 = wstack.enter_context(nc.sbuf_tensor("wsb16", [128, 2, 2048], BF16, side="right"))
            qcT = be(nc.sbuf_tensor("qcT", [128, 4 * TSL], BF16))
            qiT = be(nc.sbuf_tensor("qiT", [64, 4 * TSL], BF16))
            wihc = be(nc.sbuf_tensor("wihc", [128, 8 * 4], F32))

            def stream_w(name, mi, nk, parity, extra_idx=None):
                src = ap[name]
                if extra_idx is not None:
                    src = src[extra_idx]
                src = src.rearrange("m p k c -> m p (k c)")[mi]
                Sy.wait_ge(pe.h, pe.n)
                ldw.inc(Sy.dma_start(out=wsb[:, parity, 0:nk * 128], in_=src), 16)

            def conv_w(nk, parity):
                dv.inc(V.tensor_copy(out=wsb16[:, parity, 0:nk * 128],
                                     in_=wsb[:, parity, 0:nk * 128]))

            def wtile(parity, k):
                return wsb16[:, parity, 128 * k:128 * (k + 1)]

            barrier()
            with ExitStack() as b1:
                pj = b1.enter_context(nc.psum_tensor("pj", [128, 3 * 512], F32))
                stream_w("wsmall", 0, 16, 0)
                for m in range(6):
                    if m + 1 < 6:
                        stream_w("wsmall", m + 1, 16, (m + 1) % 2)
                    V.wait_ge(ldw.h, 16 * (m + 1))
                    conv_w(16, m % 2)
                    Te.wait_ge(dv.h, dv.n)
                    if m > 0:
                        Te.wait_ge(dv.h, dv.n)
                    for ntile in range(3):
                        ncols = 512 if ntile < 2 else 256
                        for k in range(16):
                            inst = Te.matmul(pj[:, 512 * ntile:512 * ntile + ncols],
                                             lhsT=wtile(m % 2, k),
                                             rhs=xlnT[:, k * NT + 512 * ntile:
                                                      k * NT + 512 * ntile + ncols],
                                             start=(k == 0), stop=(k == 15))
                    pe.inc(inst)
                    V.wait_ge(pe.h, pe.n)
                    psrc = pj[:, 0:NT]
                    dstv = projT[:, m * NT:(m + 1) * NT]
                    if m == 1:
                        bb = C["kvcb"][:, :].unsqueeze(1)
                        inst = V.tensor_tensor(
                            out=dstv.rearrange("p (nb mm) -> p nb mm", mm=16),
                            in0=psrc.rearrange("p (nb mm) -> p nb mm", mm=16),
                            in1=bb.broadcast_to([128, NT // 16, 16]), op=ALU.add)
                    elif m == 5:
                        V.tensor_copy(out=dstv[64:128, :], in_=psrc[64:128, :])
                        bb = C["idxb"][:, :].unsqueeze(1)
                        inst = V.tensor_tensor(
                            out=dstv[0:64, :].rearrange("p (nb mm) -> p nb mm", mm=16),
                            in0=psrc[0:64, :].rearrange("p (nb mm) -> p nb mm", mm=16),
                            in1=bb.broadcast_to([64, NT // 16, 16]), op=ALU.add)
                    else:
                        inst = V.tensor_copy(out=dstv, in_=psrc)
                    dv.inc(inst)

            barrier()
            with ExitStack() as b2:
                cmp_s = b2.enter_context(nc.sbuf_tensor("cmp_s", [128, TSL], F32))
                cmp_w = b2.enter_context(nc.sbuf_tensor("cmp_w", [128, 64], F32))
                ckv_o = b2.enter_context(nc.sbuf_tensor("ckv_o", [128, 64], F32))
                kc_o = b2.enter_context(nc.sbuf_tensor("kc_o", [64, 64], F32))

                def compress(chc, r0c, chz, r0z, rows, out_sb):
                    cview = projT[r0c:r0c + rows, chc * NT + BND:(chc + 1) * NT] \
                        .rearrange("p (nb mm) -> p nb mm", mm=16)
                    zview = projT[r0z:r0z + rows, chz * NT + BND:(chz + 1) * NT] \
                        .rearrange("p (nb mm) -> p nb mm", mm=16)
                    zm = cmp_w[r0z:r0z + rows, 0:64]
                    sview = cmp_s[r0z:r0z + rows, :].rearrange("p (nb mm) -> p nb mm", mm=16)
                    V.tensor_reduce(out=zm, in_=zview, axis=AX.X, op=ALU.max)
                    vspace()
                    dv.inc(V.tensor_tensor(
                        out=sview, in0=zview,
                        in1=zm.unsqueeze(2).broadcast_to([rows, 64, 16]),
                        op=ALU.subtract))
                    Sc.wait_ge(dv.h, dv.n)
                    ac.inc(Sc.activation(out=sview, in_=sview, func=AF.Exp,
                                         bias=0.0, scale=1.0))
                    V.wait_ge(ac.h, ac.n)
                    V.tensor_reduce(out=zm, in_=sview, axis=AX.X, op=ALU.add)
                    V.tensor_tensor(out=sview, in0=sview, in1=cview, op=ALU.mult)
                    V.reciprocal(out=zm, in_=zm)
                    V.tensor_reduce(out=out_sb, in_=sview, axis=AX.X, op=ALU.add)
                    vspace()
                    dv.inc(V.tensor_tensor(out=out_sb, in0=out_sb, in1=zm, op=ALU.mult))

                compress(0, 0, 1, 0, 128, ckv_o[:, :])
                compress(4, 0, 5, 0, 64, kc_o[:, :])

                V.memset(cmp_s[:, 0:32], 0.0)
                dv.inc(V.tensor_copy(out=cmp_s[:, 32:64], in_=cmp_s[:, 0:32]))
                Sy.wait_ge(dv.h, dv.n)
                ldw.inc(Sy.dma_start(out=cc_in[:, 96:128], in_=cmp_s[:, 0:32]), 16)
                ldw.inc(Sy.dma_start(out=cc_in[:, 0:64], in_=ckv_o[:, :]), 16)
                ldw.inc(Sy.dma_start(
                    out=cc_in[:, 64:96].rearrange("(hh p) l -> p hh l", hh=2),
                    in_=kc_o[:, :].rearrange("p (hh l) -> p hh l", hh=2)), 16)
                Gp.wait_ge(ldw.h, ldw.n)
                if NO_COLLECTIVE:
                    gp.inc(Gp.dma_start(out=cc_out[0:128, :], in_=cc_in), 16)
                else:
                    gp.inc(Gp.collective_compute(
                        "AllGather", ALU.bypass, replica_groups=[list(range(NCORES))],
                        ins=[cc_in.opt()], outs=[cc_out.opt()]), 1)
                Sy.wait_ge(gp.h, gp.n)
                ldw.inc(Sy.dma_start(
                    out=ckvT[:, :].rearrange("p (r f) -> p r f", r=8),
                    in_=cc_out.rearrange("(r p) f -> p r f", p=128)[:, :, 0:64]), 16)
                ldw.inc(Sy.dma_start(
                    out=kcT[:, :].rearrange("p (r hh l) -> p r hh l", r=8, hh=2),
                    in_=cc_out[:, 64:96].rearrange("(r hh p) l -> p r hh l", r=8, hh=2)), 16)
                GATHER_LD = ldw.n

            barrier()
            with ExitStack() as b3:
                pq = b3.enter_context(nc.psum_tensor("pq", [128, 2, 512], F32))
                psums = b3.enter_context(nc.psum_tensor("psums", [1, 512], F32))
                pbc = b3.enter_context(nc.psum_tensor("pbc", [128, 512], F32))
                qr_s = b3.enter_context(nc.sbuf_tensor("qr_s", [128, 512], F32))
                tmpA = b3.enter_context(nc.sbuf_tensor("tmpA", [128, 512], F32))
                tmpB = b3.enter_context(nc.sbuf_tensor("tmpB", [128, 512], F32))
                qsq = b3.enter_context(nc.sbuf_tensor("qsq", [128, 512], BF16))

                for m in range(4):
                    stream_w("wqc", m, 16, m % 2)
                    V.wait_ge(ldw.h, ldw.n)
                    conv_w(16, m % 2)
                    Te.wait_ge(dv.h, dv.n)
                    for nt in range(2):
                        for k in range(16):
                            inst = Te.matmul(pq[:, nt, :], lhsT=wtile(m % 2, k),
                                             rhs=xlnT[:, k * NT + BND + 512 * nt:
                                                      k * NT + BND + 512 * (nt + 1)],
                                             start=(k == 0), stop=(k == 15))
                    pe.inc(inst)
                    V.wait_ge(pe.h, pe.n)
                    dv.inc(V.tensor_copy(out=qcT[:, m * TSL:(m + 1) * TSL],
                                         in_=pq[:, :, :].rearrange("p a b -> p (a b)")))

                for hh in range(H):
                    stream_w("wqup", hh, 4, hh % 2)
                    V.wait_ge(ldw.h, ldw.n)
                    conv_w(4, hh % 2)
                    for th in range(2):
                        tq = slice(512 * th, 512 * (th + 1))
                        sl = slice(BND + 512 * th, BND + 512 * (th + 1))
                        Te.wait_ge(dv.h, dv.n)
                        for k in range(4):
                            inst = Te.matmul(pq[:, 0, :], lhsT=wtile(hh % 2, k),
                                             rhs=qcT[:, k * TSL + 512 * th:
                                                     k * TSL + 512 * (th + 1)],
                                             start=(k == 0), stop=(k == 3))
                        pe.inc(inst)
                        V.wait_ge(pe.h, pe.n)
                        P = pq[:, 0, :]
                        cs_, sn_ = C["cos2"][:, sl], C["sin2"][:, sl]
                        V.tensor_tensor(out=tmpA[0:32, :], in0=P[0:32, :],
                                        in1=cs_[0:32, :], op=ALU.mult)
                        V.tensor_tensor(out=tmpB[0:32, :], in0=P[32:64, :],
                                        in1=sn_[32:64, :], op=ALU.mult)
                        V.tensor_tensor(out=tmpA[32:64, :], in0=P[32:64, :],
                                        in1=cs_[32:64, :], op=ALU.mult)
                        V.tensor_tensor(out=tmpB[32:64, :], in0=P[0:32, :],
                                        in1=sn_[0:32, :], op=ALU.mult)
                        V.tensor_tensor(out=qr_s[0:32, :], in0=tmpA[0:32, :],
                                        in1=tmpB[0:32, :], op=ALU.subtract)
                        V.tensor_tensor(out=qr_s[32:64, :], in0=tmpB[32:64, :],
                                        in1=tmpA[32:64, :], op=ALU.add)
                        dv.inc(V.tensor_copy(out=qr_s[64:128, :], in_=P[64:128, :]))
                        Sc.wait_ge(dv.h, dv.n)
                        ac.inc(Sc.activation(out=qsq[:], in_=qr_s[:], func=AF.Square,
                                             bias=0.0, scale=1.0))
                        Te.wait_ge(ac.h, ac.n)
                        pe.inc(Te.matmul(psums[:, :], lhsT=C["onescb"][:, :], rhs=qsq[:],
                                         start=True, stop=True))
                        Sc.wait_ge(pe.h, pe.n)
                        ac.inc(Sc.activation(out=rowA[:, 0:512], in_=psums[:, :],
                                             func=AF.Sqrt, bias=C["actc"][0:1, 2:3],
                                             scale=C["actc"][0:1, 3:4]))
                        V.wait_ge(ac.h, ac.n)
                        dv.inc(V.reciprocal(out=rowB[:, 0:512], in_=rowA[:, 0:512]))
                        Te.wait_ge(dv.h, dv.n)
                        pe.inc(Te.matmul(pbc[:, :], lhsT=C["onesrf"][:, :],
                                         rhs=rowB[:, 0:512], start=True, stop=True))
                        V.wait_ge(pe.h, pe.n)
                        dv.inc(V.scalar_tensor_tensor(
                            out=qT[:, hh * TSL + 512 * th: hh * TSL + 512 * (th + 1)],
                            in0=qr_s[:], scalar=qwk[:, 0:1], in1=pbc[:, :],
                            op0=ALU.mult, op1=ALU.mult))

            barrier()
            wstack.close()
            with ExitStack() as b4:
                psums4 = b4.enter_context(nc.psum_tensor("psums4", [1, 512], F32))
                pbc4 = b4.enter_context(nc.psum_tensor("pbc4", [128, 512], F32))
                ptr_b = b4.enter_context(nc.psum_tensor("ptr_b", [128, 512], BF16))
                ptr_f = b4.enter_context(nc.psum_tensor("ptr_f", [128, 512], F32))
                pqi = b4.enter_context(nc.psum_tensor("pqi", [128, 2, 512], F32))
                ksq = b4.enter_context(nc.sbuf_tensor("ksq", [128, NT], BF16))
                knf = projT[:, 0 * NT:1 * NT]
                kA = projT[:, 1 * NT:2 * NT]
                vbf = b4.enter_context(nc.sbuf_tensor("vbf", [128, NT], BF16))
                dqb = b4.enter_context(nc.sbuf_tensor("dqb", [64, TSL], BF16))
                wiub = b4.enter_context(nc.sbuf_tensor("wiub", [64, 256], BF16))

                kraw = projT[:, 2 * NT:3 * NT]
                dv.inc(V.tensor_copy(out=vbf[:, :], in_=projT[:, 3 * NT:4 * NT]))
                kB = projT[:, 3 * NT:4 * NT]
                ac.inc(Sc.activation(out=ksq[:, :], in_=kraw, func=AF.Square,
                                     bias=0.0, scale=1.0))
                for ntile in range(3):
                    ncols = 512 if ntile < 2 else 256
                    Te.wait_ge(ac.h, ac.n)
                    pe.inc(Te.matmul(psums4[:, 0:ncols], lhsT=C["onescb"][:, :],
                                     rhs=ksq[:, 512 * ntile:512 * ntile + ncols],
                                     start=True, stop=True))
                    Sc.wait_ge(pe.h, pe.n)
                    ac.inc(Sc.activation(out=rowA[:, 512 * ntile:512 * ntile + ncols],
                                         in_=psums4[:, 0:ncols], func=AF.Sqrt,
                                         bias=C["actc"][0:1, 2:3], scale=C["actc"][0:1, 3:4]))
                V.wait_ge(ac.h, ac.n)
                dv.inc(V.reciprocal(out=rowB[:, 0:NT], in_=rowA[:, 0:NT]))
                for ntile in range(3):
                    ncols = 512 if ntile < 2 else 256
                    Te.wait_ge(dv.h, dv.n)
                    pe.inc(Te.matmul(pbc4[:, 0:ncols], lhsT=C["onesrf"][:, :],
                                     rhs=rowB[:, 512 * ntile:512 * ntile + ncols],
                                     start=True, stop=True))
                    V.wait_ge(pe.h, pe.n)
                    dv.inc(V.scalar_tensor_tensor(
                        out=knf[:, 512 * ntile:512 * ntile + ncols],
                        in0=kraw[:, 512 * ntile:512 * ntile + ncols],
                        scalar=C["knw"][:, 0:1], in1=pbc4[:, 0:ncols],
                        op0=ALU.mult, op1=ALU.mult))
                V.tensor_tensor(out=kA[0:32, :], in0=knf[0:32, :], in1=C["cos2"][0:32, :],
                                op=ALU.mult)
                V.tensor_tensor(out=kB[0:32, :], in0=knf[32:64, :], in1=C["sin2"][32:64, :],
                                op=ALU.mult)
                V.tensor_tensor(out=kA[32:64, :], in0=knf[32:64, :], in1=C["cos2"][32:64, :],
                                op=ALU.mult)
                V.tensor_tensor(out=kB[32:64, :], in0=knf[0:32, :], in1=C["sin2"][0:32, :],
                                op=ALU.mult)
                V.tensor_tensor(out=kT[0:32, :], in0=kA[0:32, :], in1=kB[0:32, :],
                                op=ALU.subtract)
                V.tensor_tensor(out=kT[32:64, :], in0=kB[32:64, :], in1=kA[32:64, :],
                                op=ALU.add)
                V.tensor_copy(out=kT[64:128, :], in_=knf[64:128, :])
                Te.wait_ge(dv.h, dv.n)
                for i in range(10):
                    par = i % 4
                    if par == 0:
                        if i > 0:
                            Te.wait_ge(dv.h, dv.n)
                    inst = Te.transpose(ptr_b[:, par * 128:(par + 1) * 128],
                                        in_=vbf[:, 128 * i:128 * (i + 1)],
                                        identity=C["idb"][:, :])
                    pe.inc(inst)
                    if par == 3 or i == 9:
                        V.wait_ge(pe.h, pe.n)
                        for u in range(par + 1):
                            ii = i - par + u
                            inst = V.tensor_copy(out=vrows[:, 128 * ii:128 * (ii + 1)],
                                                 in_=ptr_b[:, u * 128:(u + 1) * 128])
                        dv.inc(inst)
                Sc.wait_ge(ldw.h, GATHER_LD)
                V.wait_ge(ldw.h, GATHER_LD)
                dv.inc(V.tensor_copy(out=kcTb[:, :], in_=kcT[:, :]))
                ac.inc(Sc.activation(out=ksq[:, 0:NB], in_=ckvT[:, :], func=AF.Square,
                                     bias=0.0, scale=1.0))
                Te.wait_ge(ac.h, ac.n)
                pe.inc(Te.matmul(psums4[:, :], lhsT=C["onescb"][:, :],
                                 rhs=ksq[:, 0:NB], start=True, stop=True))
                Sc.wait_ge(pe.h, pe.n)
                ac.inc(Sc.activation(out=rowA[:, 0:NB], in_=psums4[:, :], func=AF.Sqrt,
                                     bias=C["actc"][0:1, 2:3], scale=C["actc"][0:1, 3:4]))
                V.wait_ge(ac.h, ac.n)
                dv.inc(V.reciprocal(out=rowB[:, 0:NB], in_=rowA[:, 0:NB]))
                Te.wait_ge(dv.h, dv.n)
                pe.inc(Te.matmul(pbc4[:, :], lhsT=C["onesrf"][:, :], rhs=rowB[:, 0:NB],
                                 start=True, stop=True))
                V.wait_ge(pe.h, pe.n)
                dv.inc(V.scalar_tensor_tensor(out=kvnT[:, :], in0=ckvT[:, :],
                                              scalar=C["knw"][:, 0:1], in1=pbc4[:, :],
                                              op0=ALU.mult, op1=ALU.mult))
                Te.wait_ge(dv.h, dv.n)
                for u in range(4):
                    inst = Te.transpose(ptr_b[:, u * 128:(u + 1) * 128],
                                        in_=kvnT[:, 128 * u:128 * (u + 1)],
                                        identity=C["idb"][:, :])
                pe.inc(inst)
                V.wait_ge(pe.h, pe.n)
                dv.inc(V.tensor_copy(out=kvnr[:, :], in_=ptr_b[:, :]))
                V.tensor_copy(out=wiub[:, :], in_=C["wiuq"][:, :])
                dv.inc(V.tensor_copy(out=dqb[:, :],
                                     in_=projT[64:128, 4 * NT + BND:5 * NT]))
                for mq in range(2):
                    Te.wait_ge(dv.h, dv.n)
                    for nt in range(2):
                        inst = Te.matmul(pqi[:, nt, :],
                                         lhsT=wiub[:, 128 * mq:128 * (mq + 1)],
                                         rhs=dqb[:, 512 * nt:512 * (nt + 1)],
                                         start=True, stop=True)
                    pe.inc(inst)
                    V.wait_ge(pe.h, pe.n)
                    V.tensor_copy(out=qiT[:, (2 * mq) * TSL:(2 * mq + 1) * TSL],
                                  in_=pqi[0:64, :, :].rearrange("p a b -> p (a b)"))
                    dv.inc(V.tensor_copy(out=qiT[:, (2 * mq + 1) * TSL:(2 * mq + 2) * TSL],
                                         in_=pqi[64:128, :, :].rearrange("p a b -> p (a b)")))
                for i in range(8):
                    Te.wait_ge(dv.h, dv.n)
                    pe.inc(Te.transpose(ptr_f[:, 0:128],
                                        in_=projT[:, 5 * NT + BND + 128 * i:
                                                  5 * NT + BND + 128 * (i + 1)],
                                        identity=C["idf"][:, :]))
                    V.wait_ge(pe.h, pe.n)
                    dv.inc(V.tensor_copy(out=wihc[:, 4 * i:4 * (i + 1)],
                                         in_=ptr_f[:, 64:68]))

            tap("projT", projT[:, :])
            tap("qcT", qcT[:, :])
            tap("ckvT", ckvT[:, :])
            tap("kcT", kcT[:, :])
            tap("qiT", qiT[:, :])
            tap("wihc", wihc[:, :])
            barrier()
            with ExitStack() as b5:
                phs = b5.enter_context(nc.psum_tensor("phs", [128, 512], F32))
                ptr_b5 = b5.enter_context(nc.psum_tensor("ptr_b5", [128, 512], BF16))
                isc = b5.enter_context(nc.sbuf_tensor("isc", [128, 2, NB], F32))
                work = b5.enter_context(nc.sbuf_tensor("work", [128, 2, NB], F32))
                seltmp = b5.enter_context(nc.sbuf_tensor("seltmp", [128, 2, NB], F32))
                selm = b5.enter_context(nc.sbuf_tensor("selm", [128, 2, NB], U8))
                s8 = b5.enter_context(nc.sbuf_tensor("s8", [128, 2, 8], F32))
                Btile = b5.enter_context(nc.sbuf_tensor("Btile", [128, 2, NB], BF16))

                for pair in range(4):
                    for u in (0, 1):
                        tile = 2 * pair + u
                        tsl_ = slice(128 * tile, 128 * (tile + 1))
                        for h4 in range(4):
                            Te.wait_ge(dv.h, dv.n)
                            pe.inc(Te.matmul(
                                phs[:, :],
                                lhsT=qiT[:, h4 * TSL + 128 * tile:
                                         h4 * TSL + 128 * (tile + 1)],
                                rhs=kcTb[:, :], start=True, stop=True))
                            V.wait_ge(pe.h, pe.n)
                            if h4 == 0:
                                dv.inc(V.tensor_scalar(
                                    out=isc[:, u, :], in0=phs[:, :], scalar1=0.0,
                                    scalar2=wihc[:, 4 * tile + h4:4 * tile + h4 + 1],
                                    op0=ALU.max, op1=ALU.mult))
                            else:
                                V.tensor_scalar(
                                    out=seltmp[:, u, :], in0=phs[:, :], scalar1=0.0,
                                    scalar2=wihc[:, 4 * tile + h4:4 * tile + h4 + 1],
                                    op0=ALU.max, op1=ALU.mult)
                                vspace()
                                dv.inc(V.tensor_tensor(out=isc[:, u, :], in0=isc[:, u, :],
                                                       in1=seltmp[:, u, :], op=ALU.add))
                    for u in (0, 1):
                        tile = 2 * pair + u
                        V.tensor_tensor(out=work[:, u, :], in0=isc[:, u, :],
                                        in1=C["ladder"][:, :], op=ALU.add)
                        V.tensor_scalar(out=selm[:, u, :], in0=C["niota"][:, :],
                                        scalar1=C["nvcol"][:, tile:tile + 1],
                                        scalar2=None, op0=ALU.is_ge)
                    for u in (0, 1):
                        V.copy_predicated(out=work[:, u, :], mask=selm[:, u, :],
                                          data=C["neglad"][:, :])
                    for r in range(8):
                        for u in (0, 1):
                            V.max(out=s8[:, u, :], in_=work[:, u, :])
                        if r < 7:
                            for u in (0, 1):
                                V.tensor_scalar(out=selm[:, u, :], in0=work[:, u, :],
                                                scalar1=s8[:, u, 7:8], scalar2=None,
                                                op0=ALU.is_ge)
                            for u in (0, 1):
                                V.copy_predicated(out=work[:, u, :], mask=selm[:, u, :],
                                                  data=C["neglad"][:, :])
                    for u in (0, 1):
                        tile = 2 * pair + u
                        V.tensor_scalar(out=seltmp[:, u, :], in0=C["niota"][:, :],
                                        scalar1=C["nvcol"][:, tile:tile + 1],
                                        scalar2=None, op0=ALU.is_lt)
                        V.tensor_tensor(out=work[:, u, :], in0=isc[:, u, :],
                                        in1=C["ladder"][:, :], op=ALU.add)
                    for u in (0, 1):
                        tile = 2 * pair + u
                        V.scalar_tensor_tensor(out=Btile[:, u, :], in0=work[:, u, :],
                                               scalar=s8[:, u, 7:8], in1=seltmp[:, u, :],
                                               op0=ALU.is_ge, op1=ALU.mult)
                    for u in (0, 1):
                        tile = 2 * pair + u
                        dv.inc(V.tensor_scalar(out=Btile[:, u, 0:1], in0=Btile[:, u, 0:1],
                                               scalar1=C["f0col"][:, tile:tile + 1],
                                               scalar2=None, op0=ALU.add))
                    for u in (0, 1):
                        tile = 2 * pair + u
                        Te.wait_ge(dv.h, dv.n)
                        for j in range(4):
                            inst = Te.transpose(ptr_b5[:, j * 128:(j + 1) * 128],
                                                in_=Btile[:, u, 128 * j:128 * (j + 1)],
                                                identity=C["idb"][:, :])
                        pe.inc(inst)
                        V.wait_ge(pe.h, pe.n)
                        for j in range(4):
                            inst = V.tensor_copy(
                                out=BTm[:, j * TSL + 128 * tile: j * TSL + 128 * (tile + 1)],
                                in_=ptr_b5[:, j * 128:(j + 1) * 128])
                        dv.inc(inst)

        tap("xlnT", xlnT[:, :])
        tap("qT", qT[:, :])
        tap("kT", kT[:, :])
        tap("kvnT", kvnT[:, :])
        tap("BTm", BTm[:, :])
        barrier()
        abctx.close()
        attnT = ce0(nc.sbuf_tensor("attnT", [128, H * TSL], BF16))

        with ExitStack() as cctx:
            ce = cctx.enter_context
            psC = ce(nc.psum_tensor("psC", [128, 4 * 512], F32))
            psAV = ce(nc.psum_tensor("psAV", [128, 512], F32))
            psS = ce(nc.psum_tensor("psS", [1, 512], F32))
            psB = ce(nc.psum_tensor("psB", [128, 512], F32))
            e_bf = ce(nc.sbuf_tensor("e_bf", [128, 4 * 512], BF16))
            pB = ce(nc.sbuf_tensor("pB", [128, 4 * 512], BF16))
            rsb = ce(nc.sbuf_tensor("rsb", [128, 512], F32))
            tm_bf = ce(nc.sbuf_tensor("tm_bf", [128, 512], BF16))

            for hh in range(H):
                for th in range(2):
                    tq = slice(512 * th, 512 * (th + 1))
                    Te.wait_ge(ac.h, ac.n)
                    Te.wait_ge(dv.h, dv.n)
                    for j in range(4):
                        inst = Te.matmul(psC[:, j * 512:(j + 1) * 512],
                                         lhsT=kvnT[:, 128 * j:128 * (j + 1)],
                                         rhs=qT[:, hh * TSL + 512 * th:
                                                hh * TSL + 512 * (th + 1)],
                                         start=True, stop=True)
                    pe.inc(inst)
                    Sc.wait_ge(pe.h, pe.n)
                    ac.inc(Sc.activation(out=e_bf[:, :], in_=psC[:, :], func=AF.Exp,
                                         bias=0.0, scale=1.0))
                    V.wait_ge(ac.h, ac.n)
                    dv.inc(V.tensor_tensor(
                        out=pB[:, :].rearrange("p (j f) -> p j f", j=4),
                        in0=e_bf[:, :].rearrange("p (j f) -> p j f", j=4),
                        in1=BTm[:, :].rearrange("p (j f) -> p j f", j=4)[:, :, tq],
                        op=ALU.mult))
                    Te.wait_ge(dv.h, dv.n)
                    for j in range(4):
                        inst = Te.matmul(psS[:, :], lhsT=C["onescb"][:, :],
                                         rhs=pB[:, j * 512:(j + 1) * 512],
                                         start=(j == 0), stop=(j == 3))
                    pe.inc(inst)
                    V.wait_ge(pe.h, pe.n)
                    V.reciprocal(out=rowA[:, 0:512], in_=psS[:, :])
                    vspace()
                    dv.inc(V.tensor_tensor(out=rowB[:, 0:512], in0=rowA[:, 0:512],
                                           in1=C["zrow"][:, tq], op=ALU.mult))
                    Te.wait_ge(dv.h, dv.n)
                    pe.inc(Te.matmul(psB[:, :], lhsT=C["onesrf"][:, :],
                                     rhs=rowB[:, 0:512], start=True, stop=True))
                    for j in range(4):
                        inst = Te.matmul(psAV[:, :], lhsT=kvnr[:, 128 * j:128 * (j + 1)],
                                         rhs=pB[:, j * 512:(j + 1) * 512],
                                         start=(j == 0), stop=(j == 3))
                    pe.inc(inst)
                    V.wait_ge(pe.h, pe.n)
                    V.tensor_copy(out=rsb[:, :], in_=psB[:, :])
                    vspace()
                    dv.inc(V.scalar_tensor_tensor(
                        out=attnT[:, hh * TSL + 512 * th: hh * TSL + 512 * (th + 1)],
                        in0=psAV[:, :], scalar=1.0, in1=rsb[:, :],
                        op0=ALU.mult, op1=ALU.mult))

            for hh in range(H):
                for wb in range(4):
                    tq = slice(WIN * wb, WIN * (wb + 1))
                    Te.wait_ge(ac.h, ac.n)
                    Te.wait_ge(dv.h, dv.n)
                    for j in range(4):
                        inst = Te.matmul(psC[:, j * WIN:(j + 1) * WIN],
                                         lhsT=kT[:, WIN * wb + 128 * j:
                                                 WIN * wb + 128 * (j + 1)],
                                         rhs=qT[:, hh * TSL + WIN * wb:
                                                hh * TSL + WIN * (wb + 1)],
                                         start=True, stop=True)
                    pe.inc(inst)
                    Sc.wait_ge(pe.h, pe.n)
                    ac.inc(Sc.activation(out=e_bf[:, 0:4 * WIN], in_=psC[:, 0:4 * WIN],
                                         func=AF.Exp, bias=0.0, scale=1.0))
                    V.wait_ge(ac.h, ac.n)
                    msk = C["bw0"] if wb == 0 else C["bwg"]
                    dv.inc(V.tensor_tensor(out=pB[:, 0:4 * WIN], in0=e_bf[:, 0:4 * WIN],
                                           in1=msk[:, :], op=ALU.mult))
                    Te.wait_ge(dv.h, dv.n)
                    for j in range(4):
                        inst = Te.matmul(psS[:, 0:WIN], lhsT=C["onescb"][:, :],
                                         rhs=pB[:, j * WIN:(j + 1) * WIN],
                                         start=(j == 0), stop=(j == 3))
                    pe.inc(inst)
                    V.wait_ge(pe.h, pe.n)
                    dv.inc(V.reciprocal(out=rowB[:, 0:WIN], in_=psS[:, 0:WIN]))
                    Te.wait_ge(dv.h, dv.n)
                    pe.inc(Te.matmul(psB[:, 0:WIN], lhsT=C["onesrf"][:, :],
                                     rhs=rowB[:, 0:WIN], start=True, stop=True))
                    for j in range(4):
                        inst = Te.matmul(psAV[:, 0:WIN],
                                         lhsT=vrows[:, (2 * wb + j) * 128:
                                                    (2 * wb + j + 1) * 128],
                                         rhs=pB[:, j * WIN:(j + 1) * WIN],
                                         start=(j == 0), stop=(j == 3))
                    pe.inc(inst)
                    V.wait_ge(pe.h, pe.n)
                    V.tensor_copy(out=rsb[:, 0:WIN], in_=psB[:, 0:WIN])
                    vspace()
                    V.scalar_tensor_tensor(out=tm_bf[:, 0:WIN], in0=psAV[:, 0:WIN],
                                           scalar=1.0, in1=rsb[:, 0:WIN],
                                           op0=ALU.mult, op1=ALU.mult)
                    vspace()
                    dst = attnT[:, hh * TSL + WIN * wb: hh * TSL + WIN * (wb + 1)]
                    dv.inc(V.tensor_tensor(out=dst, in0=dst, in1=tm_bf[:, 0:WIN],
                                           op=ALU.add))

        tap("attnT", attnT[:, :])
        barrier()
        e1ctx = ExitStack()
        grctx = ExitStack()
        if True:
            e1 = e1ctx.enter_context
            goT = grctx.enter_context(nc.sbuf_tensor("goT", [128, H * TSL], BF16, side="right"))
            with ExitStack() as e1a:
                wsb2 = e1a.enter_context(nc.sbuf_tensor("wsb2", [128, 2, 2048], F32))
                wsb2_16 = e1a.enter_context(nc.sbuf_tensor("wsb2_16", [128, 2, 2048], BF16))
                psE = e1a.enter_context(nc.psum_tensor("psE", [128, 2, 512], F32))

                def stream2(name, mi, nk, parity, extra=None):
                    src = ap[name]
                    if extra is not None:
                        src = src[extra]
                    src = src.rearrange("m p k c -> m p (k c)")[mi]
                    Sy.wait_ge(pe.h, pe.n)
                    ldw.inc(Sy.dma_start(out=wsb2[:, parity, 0:nk * 128], in_=src), 16)

                def w2tile(parity, k):
                    return wsb2_16[:, parity, 128 * k:128 * (k + 1)]

                def conv2(nk, parity):
                    dv.inc(V.tensor_copy(out=wsb2_16[:, parity, 0:nk * 128],
                                         in_=wsb2[:, parity, 0:nk * 128]))

                idx = 0
                for g in range(4):
                    for mo in range(4):
                        stream2("wgroup", mo, 4, idx % 2, extra=g)
                        V.wait_ge(ldw.h, ldw.n)
                        conv2(4, idx % 2)
                        Te.wait_ge(dv.h, dv.n)
                        for nt in range(2):
                            for k in range(4):
                                inst = Te.matmul(psE[:, nt, :], lhsT=w2tile(idx % 2, k),
                                                 rhs=attnT[:, (4 * g + k) * TSL + 512 * nt:
                                                           (4 * g + k) * TSL + 512 * (nt + 1)],
                                                 start=(k == 0), stop=(k == 3))
                        pe.inc(inst)
                        V.wait_ge(pe.h, pe.n)
                        dv.inc(V.tensor_copy(
                            out=goT[:, (4 * g + mo) * TSL:(4 * g + mo + 1) * TSL],
                            in_=psE[:, :, :].rearrange("p a b -> p (a b)")))
                        idx += 1

            barrier()
            cdctx.close()

            x1sb = e1(nc.sbuf_tensor("x1sb", [128, 8, HID], F32))
            xv8 = ap["x"].rearrange("(i p) h -> i p h", p=128)
            for i in range(8):
                ldx.inc(Sy.dma_start(out=x1sb[:, i, :], in_=xv8[i + 2]), 16)
            X1_LD = ldx.n

            barrier()
            with ExitStack() as e2a:
                wsb3 = e2a.enter_context(nc.sbuf_tensor("wsb3", [128, 2, 2048], F32))
                wsb3_16 = e2a.enter_context(nc.sbuf_tensor("wsb3_16", [128, 2, 2048], BF16))
                psE2 = e2a.enter_context(nc.psum_tensor("psE2", [128, 2, 512], F32))
                psT = e2a.enter_context(nc.psum_tensor("psT", [128, 512], F32))
                aof = e2a.enter_context(nc.sbuf_tensor("aof", [128, 1024], F32))

                def w3tile(parity, k):
                    return wsb3_16[:, parity, 128 * k:128 * (k + 1)]

                V.wait_ge(ldx.h, X1_LD)
                for mo in range(16):
                    Sy.wait_ge(pe.h, pe.n)
                    src = ap["wfinal"].rearrange("m p k c -> m p (k c)")[mo]
                    ldw.inc(Sy.dma_start(out=wsb3[:, mo % 2, :], in_=src), 16)
                    V.wait_ge(ldw.h, ldw.n)
                    dv.inc(V.tensor_copy(out=wsb3_16[:, mo % 2, :],
                                         in_=wsb3[:, mo % 2, :]))
                    Te.wait_ge(dv.h, dv.n)
                    for nt in range(2):
                        for k in range(16):
                            inst = Te.matmul(psE2[:, nt, :], lhsT=w3tile(mo % 2, k),
                                             rhs=goT[:, k * TSL + 512 * nt:
                                                     k * TSL + 512 * (nt + 1)],
                                             start=(k == 0), stop=(k == 15))
                    pe.inc(inst)
                    V.wait_ge(pe.h, pe.n)
                    dv.inc(V.tensor_copy(out=aof[:, :],
                                         in_=psE2[:, :, :].rearrange("p a b -> p (a b)")))
                    for tt in range(8):
                        Te.wait_ge(dv.h, dv.n)
                        pe.inc(Te.transpose(psT[:, 0:128],
                                            in_=aof[:, 128 * tt:128 * (tt + 1)],
                                            identity=C["idf"][:, :]))
                        V.wait_ge(pe.h, pe.n)
                        dst = x1sb[:, tt, 128 * mo:128 * (mo + 1)]
                        dv.inc(V.tensor_tensor(out=dst, in0=dst, in1=psT[:, 0:128],
                                               op=ALU.add))

        tap("x1sb", x1sb[:, :, :].rearrange("p a b -> p (a b)"))
        barrier()
        grctx.close()
        wait_all_compute(Sy)
        x1v = x1d.rearrange("(i p) h -> i p h", p=128)
        for i in range(8):
            st.inc(Sy.dma_start(out=x1v[i], in_=x1sb[:, i, :]), 16)
        X1_ST = st.n

        barrier()
        xrctx = ExitStack()
        xln2T = xrctx.enter_context(nc.sbuf_tensor("xln2T", [128, 16 * TSL], BF16, side="right"))
        with ExitStack() as fctx:
            fe = fctx.enter_context
            sc2 = fe(nc.sbuf_tensor("sc2", [128, 2, HID], F32))
            st2 = fe(nc.sbuf_tensor("st2", [128, 2, 8], F32))
            xl2n = fe(nc.sbuf_tensor("xl2n", [128, 2, HID], BF16))
            ptr_c = fe(nc.psum_tensor("ptr_c", [128, 512], BF16))
            for i in range(8):
                par = i % 2
                xi = x1sb[:, i, :]
                S = lambda k: st2[:, par, k:k + 1]
                V.tensor_reduce(out=S(0), in_=xi, axis=AX.X, op=ALU.add)
                V.scalar_tensor_tensor(out=sc2[:, par, :], in0=xi, scalar=1.0, in1=xi,
                                       op0=ALU.mult, op1=ALU.mult, accum_out=S(1))
                V.tensor_scalar(out=S(2), in0=S(0), scalar1=1.0 / HID, scalar2=None,
                                op0=ALU.mult)
                V.tensor_scalar(out=S(3), in0=S(1), scalar1=1.0 / HID, scalar2=None,
                                op0=ALU.mult)
                dv.inc(V.scalar_tensor_tensor(out=S(4), in0=S(2), scalar=S(2), in1=S(3),
                                               op0=ALU.mult, op1=ALU.subtract))
                Sc.wait_ge(dv.h, dv.n)
                ac.inc(Sc.activation(out=S(6), in_=S(4), func=AF.Sqrt,
                                     bias=C["actc"][:, 0:1], scale=C["actc"][:, 1:2]))
                V.wait_ge(ac.h, ac.n)
                V.reciprocal(out=S(7), in_=S(6))
                vspace()
                dv.inc(V.tensor_scalar(out=xl2n[:, par, :], in0=xi, scalar1=S(2),
                                       scalar2=S(7), op0=ALU.subtract, op1=ALU.mult))
                for cb in range(4):
                    Te.wait_ge(dv.h, dv.n)
                    for u in range(4):
                        hck = cb * 4 + u
                        inst = Te.transpose(ptr_c[:, u * 128:(u + 1) * 128],
                                            in_=xl2n[:, par, 128 * hck:128 * (hck + 1)],
                                            identity=C["idb"][:, :])
                    pe.inc(inst)
                    V.wait_ge(pe.h, pe.n)
                    for u in range(4):
                        hck = cb * 4 + u
                        inst = V.tensor_copy(
                            out=xln2T[:, hck * TSL + 128 * i: hck * TSL + 128 * (i + 1)],
                            in_=ptr_c[:, u * 128:(u + 1) * 128])
                    dv.inc(inst)

        barrier()
        e1ctx.close()
        hc = e(nc.sbuf_tensor("hc", [128, 64 * TSL], BF16))
        with ExitStack() as gctx:
            ge = gctx.enter_context
            wgu = ge(nc.sbuf_tensor("wgu", [128, 2, 2048], U16))
            wgu16 = ge(nc.sbuf_tensor("wgu16", [128, 2, 2, 2048], BF16))
            psG = ge(nc.psum_tensor("psG", [128, 2, 2, 512], F32))
            psU = ge(nc.psum_tensor("psU", [128, 2, 2, 512], F32))
            sg = ge(nc.sbuf_tensor("sg", [128, 2, 1024], F32))

            def gutile(parity, m, k):
                return wgu16[:, parity, m, 128 * k:128 * (k + 1)]

            wgv = ap["wg_t"].rearrange("m p k c -> m p (k c)")
            wuv = ap["wu_t"].rearrange("m p k c -> m p (k c)")
            PE_MM, DV_CONV, DV_STT, AC_SILU = {}, {}, {}, {}
            ldf.inc(Gp.dma_start(out=wgu[:, 0, :], in_=wgv[0]), 16)
            ldf.inc(Gp.dma_start(out=wgu[:, 1, :], in_=wuv[0]), 16)
            V.wait_ge(ldf.h, ldf.n)
            for m in (0, 1):
                inst = V.tensor_scalar(out=wgu16[:, 0, m, :], in0=wgu[:, m, :],
                                       scalar1=-128.0, scalar2=None, op0=ALU.add)
            dv.inc(inst); DV_CONV[0] = dv.n

            for ic in range(64):
                par = ic % 2
                if ic + 1 < 64:
                    Gp.wait_ge(dv.h, DV_CONV[ic])
                    ldf.inc(Gp.dma_start(out=wgu[:, 0, :], in_=wgv[ic + 1]), 16)
                    ldf.inc(Gp.dma_start(out=wgu[:, 1, :], in_=wuv[ic + 1]), 16)
                    V.wait_ge(ldf.h, ldf.n)
                    if ic - 1 >= 0:
                        V.wait_ge(pe.h, PE_MM[ic - 1])
                    for m in (0, 1):
                        inst = V.tensor_scalar(out=wgu16[:, 1 - par, m, :],
                                               in0=wgu[:, m, :],
                                               scalar1=-128.0, scalar2=None, op0=ALU.add)
                    dv.inc(inst); DV_CONV[ic + 1] = dv.n
                Te.wait_ge(dv.h, DV_CONV[ic])
                lag = 1 if ic <= 2 else 2
                if ic - lag >= 0:
                    Te.wait_ge(dv.h, DV_STT[ic - lag])
                    Te.wait_ge(ac.h, AC_SILU[ic - lag])
                for nt in range(2):
                    for k in range(16):
                        Te.matmul(psG[:, par, nt, :], lhsT=gutile(par, 0, k),
                                  rhs=xln2T[:, k * TSL + 512 * nt:
                                            k * TSL + 512 * (nt + 1)],
                                  start=(k == 0), stop=(k == 15))
                for nt in range(2):
                    for k in range(16):
                        inst = Te.matmul(psU[:, par, nt, :], lhsT=gutile(par, 1, k),
                                         rhs=xln2T[:, k * TSL + 512 * nt:
                                                   k * TSL + 512 * (nt + 1)],
                                         start=(k == 0), stop=(k == 15))
                pe.inc(inst); PE_MM[ic] = pe.n
                Sc.wait_ge(pe.h, PE_MM[ic])
                ac.inc(Sc.activation(out=sg[:, par, :],
                                     in_=psG[:, par, :, :].rearrange("p a b -> p (a b)"),
                                     func=AF.Silu, bias=0.0,
                                     scale=C["gsc"][:, ic:ic + 1]))
                AC_SILU[ic] = ac.n
                V.wait_ge(ac.h, AC_SILU[ic])
                dv.inc(V.scalar_tensor_tensor(
                    out=hc[:, ic * TSL:(ic + 1) * TSL],
                    in0=psU[:, par, :, :].rearrange("p a b -> p (a b)"),
                    scalar=C["usc"][:, ic:ic + 1], in1=sg[:, par, :],
                    op0=ALU.mult, op1=ALU.mult))
                DV_STT[ic] = dv.n

        barrier()
        xrctx.close()
        with ExitStack() as hctx:
            he = hctx.enter_context
            wdb = he(nc.sbuf_tensor("wdb", [128, 2, 2048], U16))
            wdb16 = he(nc.sbuf_tensor("wdb16", [128, 2, 2048], BF16))
            psD = he(nc.psum_tensor("psD", [128, 2, 512], F32))
            psT2 = he(nc.psum_tensor("psT2", [128, 512], F32))
            dfs = he(nc.sbuf_tensor("dfs", [128, 2, 512], F32))
            x1tb = he(nc.sbuf_tensor("x1tb", [128, 4, HID], F32))

            wdv = ap["wd_t"]
            ov = out.rearrange("(i p) h -> i p h", p=128)
            PE_MM2, DV_CONV2 = {}, {}

            def dma_conv(tb_, g):
                gg = 64 * tb_ + g
                ho_, sub_ = divmod(g, 4)
                par_ = gg % 2
                src = wdv[ho_][:, 16 * sub_:16 * (sub_ + 1), :] \
                    .rearrange("p k c -> p (k c)")
                if gg - 2 >= 0:
                    Gp.wait_ge(pe.h, PE_MM2[gg - 2])
                ldf.inc(Gp.dma_start(out=wdb[:, par_, :], in_=src), 16)
                V.wait_ge(ldf.h, ldf.n)
                if gg - 2 >= 0:
                    V.wait_ge(pe.h, PE_MM2[gg - 2])
                dv.inc(V.tensor_scalar(out=wdb16[:, par_, :],
                                       in0=wdb[:, par_, :],
                                       scalar1=-128.0, scalar2=None, op0=ALU.add))
                DV_CONV2[gg] = dv.n

            for tb in range(2):
                Sy.wait_ge(st.h, st.n)
                Sy.wait_ge(dv.h, dv.n)
                for u in range(4):
                    ldx.inc(Sy.dma_start(out=x1tb[:, u, :], in_=x1v[4 * tb + u]), 16)
                V.wait_ge(ldx.h, ldx.n)
                DV_DFS, DV_ADD = {}, {}

                dma_conv(tb, 0)
                for ho in range(16):
                    hop = ho % 2
                    for sub in range(4):
                        g = 4 * ho + sub
                        gg = 64 * tb + g
                        if g + 1 < 64:
                            dma_conv(tb, g + 1)
                        Te.wait_ge(dv.h, DV_CONV2[gg])
                        hlag = 1 if ho <= 2 else 2
                        if sub == 0 and ho - hlag >= 0:
                            Te.wait_ge(dv.h, DV_DFS[ho - hlag])
                        for k in range(16):
                            ig = 16 * sub + k
                            inst = Te.matmul(
                                psD[:, hop, :],
                                lhsT=wdb16[:, gg % 2, 128 * k:128 * (k + 1)],
                                rhs=hc[:, ig * TSL + 512 * tb: ig * TSL + 512 * (tb + 1)],
                                start=(ig == 0), stop=(ig == 63))
                        pe.inc(inst); PE_MM2[gg] = pe.n
                    V.wait_ge(pe.h, PE_MM2[64 * tb + 4 * ho + 3])
                    dv.inc(V.tensor_scalar(out=dfs[:, hop, :], in0=psD[:, hop, :],
                                           scalar1=C["dsc"][:, ho:ho + 1],
                                           scalar2=None, op0=ALU.mult))
                    DV_DFS[ho] = dv.n
                    Te.wait_ge(dv.h, DV_DFS[ho])
                    if ho - 1 >= 0:
                        Te.wait_ge(dv.h, DV_ADD[ho - 1])
                    for u in range(4):
                        inst = Te.transpose(psT2[:, u * 128:(u + 1) * 128],
                                            in_=dfs[:, hop, 128 * u:128 * (u + 1)],
                                            identity=C["idf"][:, :])
                    pe.inc(inst)
                    V.wait_ge(pe.h, pe.n)
                    for u in range(4):
                        inst = V.tensor_tensor(out=x1tb[:, u, 128 * ho:128 * (ho + 1)],
                                               in0=x1tb[:, u, 128 * ho:128 * (ho + 1)],
                                               in1=psT2[:, u * 128:(u + 1) * 128],
                                               op=ALU.add)
                    dv.inc(inst); DV_ADD[ho] = dv.n
                Sy.wait_ge(dv.h, dv.n)
                for u in range(4):
                    st.inc(Sy.dma_start(out=ov[4 * tb + u], in_=x1tb[:, u, :]), 16)

        Sy.wait_ge(st.h, st.n)

    return nc


_NC_CACHE = {}


def kernel(**inputs):
    in_maps = _host_inputs(inputs)
    if "nc" not in _NC_CACHE:
        _NC_CACHE["nc"] = build_nc()
    nc = _NC_CACHE["nc"]
    res = run_bass_kernel_spmd(nc, in_maps, core_ids=list(range(NCORES)))
    o = np.concatenate([res.results[c]["out"] for c in range(NCORES)], 0)
    return o.reshape(1, T, HID).astype(np.float32)
